# version 17
# baseline (speedup 1.0000x reference)
import numpy as np
import ml_dtypes
import concourse.bass as bass
import concourse.mybir as mybir
from concourse.bass_utils import run_bass_kernel_spmd
from contextlib import ExitStack

F32 = mybir.dt.float32
BF16 = mybir.dt.bfloat16
FP8 = mybir.dt.float8e4
AF = mybir.ActivationFunctionType
ALU = mybir.AluOpType
AX = mybir.AxisListType
PMDR = mybir.MatmulPerfMode.DoubleRow

NCORES = 8
T = 2048
D = 2048
NCH = 16
H = 16
DH = 128
INV_SQRT_DH = float(1.0 / np.sqrt(np.float32(128.0)))

ENGINES = ("sync", "vector", "scalar", "tensor", "gpsimd")

STAGE = "full"
DEBUG = False


class Ledger:
    def __init__(self):
        self.ops = {e: [] for e in ENGINES}
        self.sem_names = []
        self.counts = {}

    def sem(self, name):
        if name not in self.counts:
            self.sem_names.append(name)
            self.counts[name] = 0
        return name

    def inc(self, name, n=1):
        self.sem(name)
        self.counts[name] += n
        return self.counts[name]

    def val(self, name):
        return self.counts.get(name, 0)

    def emit(self, eng, fn):
        self.ops[eng].append(fn)

    def wait(self, eng, name, v):
        if v is None or v <= 0:
            return
        self.sem(name)
        self.emit(eng, lambda e, s, nm=name, vv=v: e.wait_ge(s[nm], vv))

    def op(self, eng, meth, *args, inc=None, incn=1, **kw):
        v = None
        if inc is not None:
            v = self.inc(inc, incn)

        def fn(e, s, meth=meth, args=args, kw=kw, inc=inc, incn=incn):
            r = getattr(e, meth)(*args, **kw)
            if inc is not None:
                r.then_inc(s[inc], incn)
        self.emit(eng, fn)
        return v

    def barrier(self, engines=("tensor", "vector", "scalar")):
        self.sem("BAR")
        tgt = self.val("BAR") + len(engines)
        for eng in engines:
            self.inc("BAR")
            self.emit(eng, lambda e, s: e.drain().then_inc(s["BAR"], 1))
        for eng in engines:
            self.wait(eng, "BAR", tgt)


def build_program():
    nc = bass.Bass("TRN2", target_bir_lowering=False)

    x0d = nc.dram_tensor("x0", [T, D], F32, kind="ExternalInput").ap()
    wqd = nc.dram_tensor("wq", [D, D], F32, kind="ExternalInput").ap()
    wkd = nc.dram_tensor("wk", [D, D], F32, kind="ExternalInput").ap()
    wvd = nc.dram_tensor("wv", [D, D], F32, kind="ExternalInput").ap()
    wud = nc.dram_tensor("wu", [D, D], F32, kind="ExternalInput").ap()
    wod = nc.dram_tensor("wo", [D, D], F32, kind="ExternalInput").ap()
    wfd = nc.dram_tensor("wf", [D, D], F32, kind="ExternalInput").ap()
    wsd = nc.dram_tensor("ws", [1, 1], F32, kind="ExternalInput").ap()
    bd = nc.dram_tensor("bias", [1, D], F32, kind="ExternalInput").ap()
    idbf_d = nc.dram_tensor("idbf", [128, 128], BF16, kind="ExternalInput").ap()
    idf8_d = nc.dram_tensor("idf8", [128, 128], FP8, kind="ExternalInput").ap()
    onesb_d = nc.dram_tensor("onesb", [1, 128], BF16, kind="ExternalInput").ap()
    outd = nc.dram_tensor("out", [T, D], F32, kind="ExternalOutput").ap()

    x1b = nc.dram_tensor("x1buf", [T, D], F32).ap()
    x2b = nc.dram_tensor("x2buf", [T, D], F32).ap()
    wtsp = nc.dram_tensor("wtspill", [128, 16 * D], FP8).ap()
    mx1i = nc.dram_tensor("mx1i", [1, 1], F32).ap()
    mx1o = nc.dram_tensor("mx1o", [1, 1], F32).ap()
    mx3i = nc.dram_tensor("mx3i", [1, 1], F32).ap()
    mx3o = nc.dram_tensor("mx3o", [1, 1], F32).ap()
    mxstg = nc.dram_tensor("mxstg", [1, 128], F32).ap()
    kvi = nc.dram_tensor("kvi", [128, H * DH], F32).ap()
    kvo = nc.dram_tensor("kvo", [128, H * DH], F32).ap()

    L = Ledger()
    ctx = ExitStack()

    def dv():
        L.op("vector", "drain")

    YF = ctx.enter_context(nc.sbuf_tensor("YF", [128, 16 * D], BF16))
    RA = ctx.enter_context(nc.sbuf_tensor("RA", [128, 16 * D], BF16))
    RB = ctx.enter_context(nc.sbuf_tensor("RB", [128, 16 * D], BF16))
    SCR = ctx.enter_context(nc.sbuf_tensor("SCR", [128, 7168], BF16))
    idbf = ctx.enter_context(nc.sbuf_tensor("idbf_s", [128, 128], BF16))
    idf8 = ctx.enter_context(nc.sbuf_tensor("idf8_s", [128, 128], FP8))
    onesb = ctx.enter_context(nc.sbuf_tensor("onesb_s", [1, 128], BF16))
    ws_bc = ctx.enter_context(nc.sbuf_tensor("ws_bc", [128, 1], F32))
    gmax_bc = ctx.enter_context(nc.sbuf_tensor("gmax_bc", [128, 1], F32))
    stt = ctx.enter_context(nc.sbuf_tensor("stt", [128, 13 * 16], F32))
    mxv = ctx.enter_context(nc.sbuf_tensor("mxv", [128, 3], F32))
    mx11 = ctx.enter_context(nc.sbuf_tensor("mx11", [1, 2], F32))

    def SL(idx):
        return stt[:, idx * 16:(idx + 1) * 16]

    PB = [ctx.enter_context(nc.psum_tensor(f"PB{i}", [128, 512], F32))
          for i in range(8)]

    def PBb(i):
        return PB[i][:].bitcast(BF16)

    RAv = RA[:].rearrange("p (a b) -> p a b", b=D)
    RA8 = RA[:].bitcast(FP8)
    RA8u = RA8[:, 32768:65536].rearrange("p (a b) -> p a b", b=D)
    RB8 = RB[:].bitcast(FP8)
    Wnat8 = RB8[:, 32768:65536].rearrange("p (a b) -> p a b", b=D)
    YF8 = YF[:].bitcast(FP8)
    YF8v = YF8[:].rearrange("p (a b) -> p a b", b=2 * D)
    WT8_C = YF8v[:, :, 0:D]
    RBv = RB[:].rearrange("p (a b) -> p a b", b=D)
    YFv = YF[:].rearrange("p (a b) -> p a b", b=D)
    RAh = RA[:].rearrange("p (a b) -> p a b", b=1024)
    RBh = RB[:].rearrange("p (a b) -> p a b", b=1024)
    YF32 = YF[:].bitcast(F32)
    RA32 = RA[:].bitcast(F32)
    RB32 = RB[:].bitcast(F32)

    def A1_xin(sl):
        return YF32[:, sl * 2048:(sl + 1) * 2048]
    bias_stage = YF32[0:1, 8192:10240]

    def A_xin(sl):
        return RA32[:, sl * 2048:(sl + 1) * 2048]

    def A_out(sl):
        return RA32[:, (2 + sl) * 2048:(3 + sl) * 2048]

    def B5_xin(sl):
        return RA32[:, sl * 2048:(sl + 1) * 2048]

    def B5_out(sl):
        return RA32[:, (2 + sl) * 2048:(3 + sl) * 2048]

    def C_xin(sl):
        return RB32[:, sl * 2048:(sl + 1) * 2048]

    def C_out(sl):
        return RB32[:, (2 + sl) * 2048:(3 + sl) * 2048]

    q8 = SCR[:, 0:1024].bitcast(FP8)
    xqfm8s = [SCR[:, 1024:2048].bitcast(FP8),
              SCR[:, 4096:5120].bitcast(FP8)]
    y_tm2 = SCR[:, 2048:4096]
    biasrow = SCR[0:1, 5120:7168]
    k_sbs = [SCR[:, 0:1024], SCR[:, 2048:3072]]
    v_sbs = [SCR[:, 1024:2048], SCR[:, 3072:4096]]
    kvstage = SCR[:, 4096:6144].bitcast(F32)
    SCq = SCR[:, 0:2048]
    SCqfm = SCR[:, 2048:4096]
    junk_b5 = SCR[:, 0:2048]
    kv_sb = SCR[:, 4096:6144]

    for s in ("LX", "LW", "SO", "TP", "VP", "SP", "GP", "CC", "BAR"):
        L.sem(s)

    L.op("sync", "dma_start", out=idbf[:], in_=idbf_d, inc="LX", incn=16)
    L.op("sync", "dma_start", out=onesb[:], in_=onesb_d, inc="LX", incn=16)
    L.op("sync", "dma_start", out=mx11[0:1, 1:2], in_=wsd[0:1, 0:1],
         inc="LX", incn=16)
    L.op("sync", "dma_start", out=idf8[:], in_=idf8_d, inc="LX", incn=16)
    L.op("sync", "dma_start", out=bias_stage, in_=bd, inc="LX", incn=16)
    consts_ld = L.val("LX")

    wf_view = wfd.rearrange("(a p) i -> p a i", p=128)
    wffn_ld4 = []
    for q4 in range(4):
        L.op("gpsimd", "dma_start", out=Wnat8[:, q4 * 4:(q4 + 1) * 4, :],
             in_=wf_view[:, q4 * 4:(q4 + 1) * 4, :], inc="LW", incn=16)
        wffn_ld4.append(L.val("LW"))

    L.wait("vector", "LX", consts_ld)
    h11 = mx11[0:1, 0:1]
    ws11 = mx11[0:1, 1:2]
    L.op("vector", "tensor_tensor", h11, ws11, ws11, ALU.mult)
    dv()
    L.op("vector", "tensor_scalar", h11, h11, 1e-30, None, ALU.max)
    dv()
    L.op("vector", "reciprocal", h11, h11)
    dv()
    L.op("vector", "tensor_tensor", h11, h11, ws11, ALU.mult)
    dv()
    L.op("vector", "tensor_scalar", h11, h11, 0.5, None, ALU.mult)
    dv()
    L.op("vector", "tensor_scalar", biasrow, bias_stage, h11, None,
         ALU.mult, inc="VP")
    v_biasrow = L.val("VP")

    ones32 = SCR[:, 0:256].bitcast(F32)[0:1, 0:128]
    g11 = SCR[:, 256:260].bitcast(F32)[0:1, 0:1]

    def bcast_128(src_11, dst_128, wait_list, who_waits_next):
        for (eng, sem, vv) in wait_list:
            L.wait(eng, sem, vv)
        L.op("vector", "memset", ones32, 1.0)
        dv()
        L.op("vector", "nop", inc="VP")
        vm = L.val("VP")
        L.wait("tensor", "VP", vm)
        L.op("tensor", "matmul", PB[0][:, 0:1], ones32, src_11,
             start=True, stop=True, inc="TP")
        tm = L.val("TP")
        L.wait("vector", "TP", tm)
        L.op("vector", "tensor_copy", dst_128, PB[0][:, 0:1], inc="VP")
        vv = L.val("VP")
        dv()
        return vv

    v_wsbc = bcast_128(mx11[0:1, 1:2], ws_bc[:], [("vector", "LX", consts_ld)],
                       "scalar")

    L.wait("tensor", "LX", consts_ld)
    L.wait("tensor", "VP", v_wsbc)
    L.wait("scalar", "VP", v_wsbc)
    def PBf8(i):
        return PB[i][:].bitcast(FP8)
    s_wt_evac = 0
    for oc in range(16):
        if oc % 4 == 0:
            L.wait("tensor", "LW", wffn_ld4[oc // 4])
        L.wait("tensor", "SP", s_wt_evac)
        for ic in range(16):
            b = ic // 8
            sl8 = ic % 8
            L.op("tensor", "transpose",
                 PBf8(b)[:, sl8 * 256:(sl8 + 1) * 256:2],
                 Wnat8[:, oc, ic * 128:(ic + 1) * 128], idf8[:],
                 inc=("TP" if ic == 15 else None))
        t_wt = L.val("TP")
        L.wait("scalar", "TP", t_wt)
        for bank in range(2):
            dst = RA8u[:, bank * 8:(bank + 1) * 8, oc * 128:(oc + 1) * 128]
            src = PBf8(bank)[:, 0:2048].rearrange(
                "p (t c) -> p t c", c=256)[:, :, 0:256:2]
            L.op("scalar", "copy", dst, src,
                 inc=("SP" if bank == 1 else None))
        s_wt_evac = L.val("SP")
    t_wt_done = L.val("TP")
    s_wt_done = L.val("SP")

    L.wait("gpsimd", "SP", s_wt_done)
    L.op("gpsimd", "dma_start", out=wtsp, in_=RA8[:, 32768:65536], inc="LW", incn=16)

    xin_rel = [(0, 0), (0, 0)]
    for lc in range(NCH):
        sl = lc % 2
        L.wait("sync", "VP", xin_rel[sl][0])
        L.wait("sync", "SP", xin_rel[sl][1])
        L.op("sync", "dma_start", out=A1_xin(sl),
             in_=x0d[lc * 128:(lc + 1) * 128, :], inc="LX", incn=16)
        ld = L.val("LX")
        L.wait("vector", "LX", ld)
        L.op("vector", "tensor_reduce", SL(2)[:, lc:lc + 1], A1_xin(sl), AX.X, ALU.max)
        L.op("vector", "tensor_reduce", SL(3)[:, lc:lc + 1], A1_xin(sl), AX.X, ALU.min,
             inc="VP")
        v_done = L.val("VP")
        L.wait("scalar", "LX", ld)
        L.op("scalar", "activation", y_tm2[:, 0:2048], A1_xin(sl), AF.Square,
             accum_out=SL(1)[:, lc:lc + 1])
        L.op("scalar", "activation", y_tm2[:, 0:2048], A1_xin(sl), AF.Copy,
             accum_out=SL(0)[:, lc:lc + 1], inc="SP")
        s_done = L.val("SP")
        xin_rel[sl] = (v_done, s_done)

    def batch_ln_finalize(with_max, cc_in, cc_out):
        L.op("vector", "tensor_scalar", SL(4), SL(0), -1.0 / D, None, ALU.mult)
        dv()
        L.op("vector", "tensor_tensor", SL(7), SL(4), SL(4), ALU.mult)
        dv()
        L.op("vector", "scalar_tensor_tensor", SL(8), SL(1), 1.0 / D, SL(7),
             ALU.mult, ALU.subtract)
        dv()
        L.op("vector", "tensor_scalar", SL(8), SL(8), 1e-5, None, ALU.add)
        dv()
        L.op("vector", "reciprocal", SL(5), SL(8), inc="VP")
        v_seed = L.val("VP")
        L.wait("scalar", "VP", v_seed)
        L.op("scalar", "activation", SL(5), SL(5), AF.Sqrt, inc="SP")
        s_seed = L.val("SP")
        L.wait("vector", "SP", s_seed)
        for it in range(2):
            L.op("vector", "tensor_tensor", SL(7), SL(5), SL(5), ALU.mult)
            dv()
            L.op("vector", "tensor_tensor", SL(7), SL(7), SL(8), ALU.mult)
            dv()
            L.op("vector", "tensor_scalar", SL(7), SL(7), -0.5, 1.5, ALU.mult, ALU.add)
            dv()
            L.op("vector", "tensor_tensor", SL(5), SL(5), SL(7), ALU.mult,
                 inc=("VP" if it == 1 else None))
            dv()
        marks = {}
        marks["rstd"] = L.val("VP")
        marks["psum0_free"] = 0
        if with_max:
            L.op("vector", "tensor_tensor", SL(7), SL(2), SL(4), ALU.add)
            L.op("vector", "tensor_tensor", SL(8), SL(3), SL(4), ALU.add)
            dv()
            L.op("vector", "tensor_scalar", SL(8), SL(8), -1.0, None, ALU.mult)
            dv()
            L.op("vector", "tensor_tensor", SL(7), SL(7), SL(8), ALU.max)
            dv()
            L.op("vector", "tensor_tensor", SL(7), SL(7), SL(5), ALU.mult)
            dv()
            L.op("vector", "tensor_reduce", mxv[:, 0:1], SL(7), AX.X, ALU.max,
                 inc="VP")
            v_mx = L.val("VP")
            L.wait("sync", "VP", v_mx)
            L.op("sync", "dma_start", out=mxstg, in_=mxv[:, 0:1],
                 inc="LX", incn=16)
            g_a = L.val("LX")
            L.wait("sync", "LX", g_a)
            mxrow = SCR[:, 512:768].bitcast(F32)[0:1, 0:128]
            L.op("sync", "dma_start", out=mxrow, in_=mxstg, inc="LX", incn=16)
            g_b = L.val("LX")
            L.wait("vector", "LX", g_b)
            L.op("vector", "tensor_reduce", mx11[0:1, 0:1], mxrow,
                 AX.X, ALU.max, inc="VP")
            v_mx11 = L.val("VP")
            L.wait("sync", "VP", v_mx11)
            L.op("sync", "dma_start", out=cc_in, in_=mx11[0:1, 0:1],
                 inc="LX", incn=16)
            g0 = L.val("LX")
            L.wait("gpsimd", "LX", g0)
            L.op("gpsimd", "collective_compute", "AllReduce", ALU.max,
                 replica_groups=[list(range(NCORES))],
                 ins=[cc_in.opt()], outs=[cc_out.opt()], inc="CC")
            cc_v = L.val("CC")
            L.wait("gpsimd", "CC", cc_v)
            L.op("gpsimd", "dma_start", out=g11, in_=cc_out[0:1, 0:1],
                 inc="GP", incn=16)
            g1 = L.val("GP")
            v_gbc = bcast_128(g11, gmax_bc[:], [("vector", "GP", g1)], "vector")
            marks["psum0_free"] = v_gbc
            L.op("vector", "tensor_scalar", mxv[:, 1:2], gmax_bc[:], 1.0 / 448.0,
                 1e-7, ALU.mult, ALU.max)
            dv()
            dv()
            L.op("vector", "tensor_scalar", mxv[:, 1:2], mxv[:, 1:2], 2.0, None,
                 ALU.mult)
            dv()
            L.op("vector", "reciprocal", mxv[:, 2:3], mxv[:, 1:2])
            dv()
            L.op("vector", "tensor_scalar", SL(6), SL(5), mxv[:, 2:3], None,
                 ALU.mult, inc="VP")
            marks["rstd_s"] = L.val("VP")
        dv()
        return marks

    L.barrier()
    marks_a = batch_ln_finalize(True, mx1i, mx1o)

    pipe = {"tr": t_wt_done, "trevac": ("VP", marks_a["psum0_free"]),
            "y2t": 0, "y2evac": ("VP", 0)}

    def ffn_phase(xsrc_d, xdst_d, wt8v, xin_f, xout_f, fuse_ln2, rstd_s_mark,
                  src_store_marks, v_biasrow_mark, xin_guards=None):
        xin_rel = [0, 0]
        tb_free = {b: 0 for b in range(2, 6)}
        store_marks = [None] * NCH
        sq_marks = [0] * NCH
        pend_resid = []
        pend_post = []
        y2_pend = []
        L.wait("tensor", "VP", v_biasrow_mark)

        def load_chunk(lc):
            sl = lc % 2
            if lc < 2 and xin_guards is not None:
                for (gs, gv) in xin_guards[sl]:
                    L.wait("sync", gs, gv)
            L.wait("sync", "VP", xin_rel[sl])
            if src_store_marks is not None:
                L.wait("sync", "SO", src_store_marks[lc])
            L.op("sync", "dma_start", out=xin_f(sl),
                 in_=xsrc_d[lc * 128:(lc + 1) * 128, :], inc="LX", incn=16)
            return L.val("LX")

        def quantize(lc, ld):
            sl = lc % 2
            L.wait("vector", "LX", ld)
            if lc == 0:
                L.wait("vector", "VP", rstd_s_mark)
            L.wait("vector", "TP", pipe["tr"])
            L.op("vector", "tensor_scalar", q8[:, :], xin_f(sl),
                 SL(4)[:, lc:lc + 1], SL(6)[:, lc:lc + 1], ALU.add, ALU.mult,
                 inc="VP")
            return L.val("VP")

        def xq_transpose(lc, v_q8):
            L.wait("tensor", "VP", v_q8)
            L.wait("tensor", pipe["trevac"][0], pipe["trevac"][1])
            for ic in range(16):
                b = ic // 8
                sl8 = ic % 8
                L.op("tensor", "transpose",
                     PBf8(b)[:, sl8 * 256:(sl8 + 1) * 256:2],
                     q8[:, ic * 128:(ic + 1) * 128], idf8[:],
                     inc=("TP" if ic == 15 else None))
            pipe["tr"] = L.val("TP")
            L.wait("scalar", "TP", pipe["tr"])
            xq = xqfm8s[lc % 2]
            for bank in range(2):
                srcap = PBf8(bank)[:, 0:2048].rearrange(
                    "p (t c) -> p t c", c=256)[:, :, 0:256:2]
                dstap = xq[:, bank * 1024:(bank + 1) * 1024].rearrange(
                    "p (t c) -> p t c", c=128)
                L.op("scalar", "copy", dstap, srcap,
                     inc=("SP" if bank == 1 else None))
            pipe["trevac"] = ("SP", L.val("SP"))

        def flush_post():
            lc0, sl0, v0 = pend_post.pop(0)
            L.wait("scalar", "VP", v0)
            if fuse_ln2:
                L.wait("scalar", "TP", pipe["y2t"])
                L.op("scalar", "activation", y_tm2[:, :], xout_f(sl0), AF.Copy,
                     accum_out=SL(0)[:, lc0:lc0 + 1])
                L.op("scalar", "activation", y_tm2[:, :], xout_f(sl0), AF.Square,
                     accum_out=SL(1)[:, lc0:lc0 + 1], inc="SP")
            sq_marks[lc0] = L.val("SP")
            L.op("scalar", "dma_start", out=xdst_d[lc0 * 128:(lc0 + 1) * 128, :],
                 in_=xout_f(sl0), inc="SO", incn=16)
            store_marks[lc0] = L.val("SO")
            if not fuse_ln2:
                return None, None
            c = slice(lc0, lc0 + 1)
            L.wait("vector", "SP", sq_marks[lc0])
            L.op("vector", "tensor_scalar", SL(2)[:, c], SL(0)[:, c], -1.0 / D,
                 None, ALU.mult)
            dv()
            L.op("vector", "tensor_tensor", SL(7)[:, c], SL(2)[:, c], SL(2)[:, c],
                 ALU.mult)
            dv()
            L.op("vector", "scalar_tensor_tensor", SL(8)[:, c], SL(1)[:, c],
                 1.0 / D, SL(7)[:, c], ALU.mult, ALU.subtract)
            dv()
            L.op("vector", "tensor_scalar", SL(8)[:, c], SL(8)[:, c], 1e-5, None,
                 ALU.add)
            dv()
            L.op("vector", "reciprocal", SL(3)[:, c], SL(8)[:, c], inc="VP")
            v_seed = L.val("VP")
            L.wait("scalar", "VP", v_seed)
            L.op("scalar", "activation", SL(3)[:, c], SL(3)[:, c], AF.Sqrt,
                 inc="SP")
            s_seed = L.val("SP")
            L.wait("vector", "SP", s_seed)
            L.op("vector", "tensor_tensor", SL(7)[:, c], SL(3)[:, c],
                 SL(3)[:, c], ALU.mult)
            dv()
            L.op("vector", "tensor_tensor", SL(7)[:, c], SL(7)[:, c],
                 SL(8)[:, c], ALU.mult)
            dv()
            L.op("vector", "tensor_scalar", SL(7)[:, c], SL(7)[:, c], -0.5,
                 1.5, ALU.mult, ALU.add)
            dv()
            L.op("vector", "tensor_tensor", SL(3)[:, c], SL(3)[:, c],
                 SL(7)[:, c], ALU.mult)
            dv()
            L.op("vector", "tensor_tensor", SL(7)[:, c], SL(2)[:, c],
                 SL(3)[:, c], ALU.mult)
            dv()
            L.op("vector", "nop")
            L.op("vector", "engine_nop", inc="VP") if False else None
            L.op("vector", "tensor_copy", SL(8)[:, c], SL(7)[:, c], inc="VP")
            v_mb = L.val("VP")
            L.wait("scalar", "VP", v_mb)
            L.op("scalar", "activation", y_tm2[:, :], xout_f(sl0), AF.Identity,
                 scale=SL(3)[:, c], bias=SL(8)[:, c], inc="SP")
            s_aff = L.val("SP")
            return lc0, ("SP", s_aff)

        def flush_resid():
            lc0, sl0, banks0, t_mm0 = pend_resid.pop(0)
            L.wait("vector", "TP", t_mm0)
            if lc0 >= 2:
                L.wait("vector", "SO", store_marks[lc0 - 2])
                L.wait("vector", "SP", sq_marks[lc0 - 2])
            for ot in range(4):
                L.op("vector", "scalar_tensor_tensor",
                     xout_f(sl0)[:, ot * 512:(ot + 1) * 512], PB[banks0[ot]][:],
                     ws_bc[:], xin_f(sl0)[:, ot * 512:(ot + 1) * 512],
                     ALU.mult, ALU.add, inc="VP")
                tb_free[banks0[ot]] = L.val("VP")
            v_out = L.val("VP")
            xin_rel[sl0] = v_out
            pend_post.append((lc0, sl0, v_out))

        def emit_y2_transposes():
            lc0, v_aff = y2_pend.pop(0)
            L.wait("tensor", v_aff[0], v_aff[1])
            L.wait("tensor", pipe["y2evac"][0], pipe["y2evac"][1])
            for ic in range(16):
                b = 6 + ic // 8
                L.op("tensor", "transpose",
                     PBb(b)[:, (ic % 8) * 128:(ic % 8 + 1) * 128],
                     y_tm2[:, ic * 128:(ic + 1) * 128], idbf[:],
                     inc=("TP" if ic == 15 else None))
            t_y2 = L.val("TP")
            pipe["y2t"] = t_y2
            L.wait("scalar", "TP", t_y2)
            for bank in range(2):
                dst = YFv[:, bank * 8:(bank + 1) * 8, lc0 * 128:(lc0 + 1) * 128]
                src2 = PBb(6 + bank)[:, 0:1024].rearrange(
                    "p (t c) -> p t c", c=128)
                L.op("scalar", "copy", dst, src2,
                     inc=("SP" if bank == 1 else None))
            pipe["y2evac"] = ("SP", L.val("SP"))

        ld0 = load_chunk(0)
        vq = quantize(0, ld0)
        xq_transpose(0, vq)
        t_mm = 0
        for lc in range(NCH):
            if pend_resid:
                flush_resid()
            if lc + 1 < NCH:
                ld = load_chunk(lc + 1)
                vq = quantize(lc + 1, ld)
            if y2_pend:
                emit_y2_transposes()
            if len(pend_post) >= 2:
                lc0, v_aff = flush_post()
                if lc0 is not None:
                    y2_pend.append((lc0, v_aff))
            banks = [2 + (lc * 4 + ot) % 4 for ot in range(4)]
            L.wait("tensor", pipe["trevac"][0], pipe["trevac"][1])
            order = sorted(range(4), key=lambda ot: tb_free[banks[ot]])
            for ot in order:
                L.wait("tensor", "VP", tb_free[banks[ot]])
            xq = xqfm8s[lc % 2]
            for ot in order:
                for a in range(8):
                    stat = xq[:, (2 * a) * 128:(2 * a + 2) * 128].rearrange(
                        "p (j m) -> p j m", j=2)
                    mov = wt8v[:, 2 * a:2 * a + 2, ot * 512:(ot + 1) * 512]
                    L.op("tensor", "matmul", PB[banks[ot]][:], stat, mov,
                         start=(a == 0), stop=False, perf_mode=PMDR)
            for ot in range(4):
                L.op("tensor", "matmul", PB[banks[ot]][:], onesb[:],
                     biasrow[0:1, ot * 512:(ot + 1) * 512],
                     start=False, stop=True, inc=("TP" if ot == 3 else None))
            t_mm = L.val("TP")
            pend_resid.append((lc, lc % 2, banks, t_mm))
            if lc + 1 < NCH:
                xq_transpose(lc + 1, vq)
        while pend_resid or pend_post:
            while len(pend_post) >= 2 or (pend_post and not pend_resid):
                while y2_pend:
                    emit_y2_transposes()
                lc0, v_aff = flush_post()
                if lc0 is not None:
                    y2_pend.append((lc0, v_aff))
            if pend_resid:
                flush_resid()
        while y2_pend:
            emit_y2_transposes()
        return store_marks, t_mm, pipe["y2evac"]

    if STAGE == "A":
        ffn_phase(x0d, outd, RA8u, A_xin, A_out, False, marks_a["rstd_s"], None,
                  v_biasrow, None)
        final_so = L.val("SO")
    else:
        a2_marks, a2_last_mm, a2_tail_v = ffn_phase(
            x0d, x1b, RA8u, A_xin, A_out, True, marks_a["rstd_s"], None,
            v_biasrow, None)

        def load_w(dst_view, wd, col0, ncols, wait_sem=None, wait_v=0):
            if wait_sem is not None:
                L.wait("gpsimd", wait_sem, wait_v)
            src = wd[:, col0:col0 + ncols].rearrange("(a p) o -> p a o", p=128)
            L.op("gpsimd", "dma_start", out=dst_view, in_=src, inc="LW", incn=16)
            return L.val("LW")

        lw_wk1 = load_w(RBh[:, 0:16, :], wkd, 0, 1024, "TP", t_wt_done)
        lw_wv1 = load_w(RBh[:, 16:32, :], wvd, 0, 1024)
        L.wait("gpsimd", "SO", a2_marks[NCH - 1])
        L.wait("gpsimd", "TP", a2_last_mm)
        L.wait("gpsimd", a2_tail_v[0], a2_tail_v[1])
        lw_wk2 = load_w(RAh[:, 0:16, :], wkd, 1024, 1024)
        lw_wv2 = load_w(RAh[:, 16:32, :], wvd, 1024, 1024)

        if STAGE == "Y2":
            L.barrier()
            L.wait("gpsimd", "BAR", L.val("BAR"))
            L.op("gpsimd", "dma_start",
                 out=outd.rearrange("(a p) d -> p a d", p=128),
                 in_=YFv, inc="SO", incn=16)
            L.wait("gpsimd", "SO", L.val("SO"))
            sems = {}
            for name in L.sem_names:
                sems[name] = ctx.enter_context(nc.semaphore(name))
            block = ctx.enter_context(nc.Block())

            def runner2(eng_name):
                def run(e):
                    for fn in L.ops[eng_name]:
                        fn(e, sems)
                return run
            for eng_name in ENGINES:
                getattr(block, eng_name)(runner2(eng_name))
            ctx.close()
            return nc, sum(len(v) for v in L.ops.values())

        L.barrier()

        def b2_half(hbase, WH, lw_wait, pb67_free_v, kvstage_free_g):
            L.wait("tensor", "LW", lw_wait)
            L.wait("tensor", "VP", pb67_free_v)
            k_free = [L.val("SP")] * 2
            v_free = [L.val("VP")] * 2
            evac_marks = []

            def kv_mms(lck):
                pk = lck % 2
                sk, vv = evac_marks[lck]
                L.wait("tensor", "SP", sk)
                L.wait("tensor", "VP", vv)
                for hh in range(8):
                    L.op("tensor", "matmul",
                         PB[6 + hh // 4][:, (hh % 4) * 128:(hh % 4 + 1) * 128],
                         k_sbs[pk][:, hh * 128:(hh + 1) * 128],
                         v_sbs[pk][:, hh * 128:(hh + 1) * 128],
                         start=(lck == 0 and hh % 4 == 0), stop=(lck == NCH - 1),
                         skip_group_check=True,
                         inc=("TP" if hh == 7 else None))
                return L.val("TP")

            kv_done = {}
            for lc in range(NCH):
                pk = lc % 2
                L.wait("tensor", "SP", k_free[pk])
                for ic in range(16):
                    for ot in range(2):
                        L.op("tensor", "matmul", PB[ot][:],
                             YFv[:, ic, lc * 128:(lc + 1) * 128],
                             WH[:, ic, ot * 512:(ot + 1) * 512],
                             start=(ic == 0), stop=(ic == 15),
                             inc=("TP" if (ic == 15 and ot == 1) else None))
                t_k = L.val("TP")
                L.wait("tensor", "VP", v_free[pk])
                for ic in range(16):
                    for ot in range(2):
                        L.op("tensor", "matmul", PB[2 + ot][:],
                             YFv[:, ic, lc * 128:(lc + 1) * 128],
                             WH[:, 16 + ic, ot * 512:(ot + 1) * 512],
                             start=(ic == 0), stop=(ic == 15),
                             inc=("TP" if (ic == 15 and ot == 1) else None))
                t_v = L.val("TP")
                if lc >= 1:
                    kv_done[lc - 1] = kv_mms(lc - 1)
                L.wait("scalar", "TP", t_k)
                for ot in range(2):
                    L.op("scalar", "activation",
                         k_sbs[pk][:, ot * 512:(ot + 1) * 512],
                         PB[ot][:], AF.Relu, scale=INV_SQRT_DH,
                         inc=("SP" if ot == 1 else None))
                k_free[pk] = L.val("SP")
                L.wait("vector", "TP", t_v)
                for ot in range(2):
                    L.op("vector", "tensor_copy",
                         v_sbs[pk][:, ot * 512:(ot + 1) * 512],
                         PB[2 + ot][:], inc=("VP" if ot == 1 else None))
                v_free[pk] = L.val("VP")
                evac_marks.append((k_free[pk], v_free[pk]))
            kv_done[NCH - 1] = kv_mms(NCH - 1)
            L.wait("vector", "TP", kv_done[NCH - 1])
            L.wait("vector", "GP", kvstage_free_g)
            for bank in range(2):
                L.op("vector", "tensor_copy",
                     kvstage[:, bank * 512:(bank + 1) * 512], PB[6 + bank][:],
                     inc=("VP" if bank == 1 else None))
            v_kvst = L.val("VP")
            L.wait("gpsimd", "VP", v_kvst)
            L.op("gpsimd", "dma_start",
                 out=kvi[:, hbase * 128:hbase * 128 + 1024], in_=kvstage,
                 inc="GP", incn=16)
            return v_kvst, L.val("GP")

        v_kv1, g_kv1 = b2_half(0, RBh, lw_wv1, 0, 0)
        lw_wq = load_w(RBv, wqd, 0, 2048, "TP", L.val("TP"))
        v_kv2, g_kv2 = b2_half(8, RAh, lw_wv2, v_kv1, g_kv1)

        lw_wu4 = []
        wu_gate = L.val("TP")
        for qtr in range(4):
            dst = RAv[:, :, qtr * 512:(qtr + 1) * 512]
            lw_wu4.append(load_w(dst, wud, qtr * 512, 512,
                                 "TP" if qtr == 0 else None, wu_gate))
        L.wait("gpsimd", "GP", g_kv2)
        L.op("gpsimd", "collective_compute", "AllReduce", ALU.add,
             replica_groups=[list(range(NCORES))], ins=[kvi.opt()],
             outs=[kvo.opt()], inc="CC")
        cc_kv = L.val("CC")
        L.wait("gpsimd", "CC", cc_kv)
        L.op("gpsimd", "dma_start", out=kv_sb[:, :], in_=kvo, inc="LW", incn=16)
        lw_kv = L.val("LW")

        L.barrier()
        L.wait("tensor", "LW", lw_wq)
        prev = {"v_on": 0, "s_ss": 0, "s_u": 0, "v_r": 0, "r_evac": [0, 0],
                "t_rt": 0}
        pend_rt = []
        t_q_last = 0
        PAR = [0] + [(lc + 1) % 2 for lc in range(1, NCH)]
        RTSET = [1] + [PAR[lc] for lc in range(1, NCH)]
        def flush_rt():
            lc0, set0, v_r0 = pend_rt.pop(0)
            SPp = [0, 1, 2, 3] if set0 == 0 else [4, 5, 6, 7]
            L.wait("tensor", "VP", v_r0)
            for ic in range(16):
                L.op("tensor", "transpose",
                     PBb(SPp[2 + ic // 8])[:, (ic % 8) * 128:(ic % 8 + 1) * 128],
                     SCqfm[:, ic * 128:(ic + 1) * 128], idbf[:],
                     inc=("TP" if ic == 15 else None))
            t_rt0 = L.val("TP")
            L.wait("vector", "TP", t_rt0)
            for bank in range(2):
                dst = YFv[:, bank * 8:(bank + 1) * 8, lc0 * 128:(lc0 + 1) * 128]
                src2 = PBb(SPp[2 + bank])[:, 0:1024].rearrange(
                    "p (t c) -> p t c", c=128)
                L.op("vector", "tensor_copy", dst, src2,
                     inc=("VP" if bank == 1 else None))
            return L.val("VP")
        for lc in range(NCH):
            p = PAR[lc]
            SP_ = [0, 1, 2, 3] if p == 0 else [4, 5, 6, 7]
            SO_ = [4, 5, 6, 7] if p == 0 else [0, 1, 2, 3]
            if lc == 1:
                L.wait("tensor", "SP", prev["s_u"])
            else:
                L.wait("tensor", "VP", prev["v_on"])
                L.wait("tensor", "SP", prev["s_ss"])
            L.wait("tensor", "VP", prev["r_evac"][p])
            for b4 in range(4):
                for ic in range(16):
                    L.op("tensor", "matmul", PB[SP_[b4]][:],
                         YFv[:, ic, lc * 128:(lc + 1) * 128],
                         RBv[:, ic, b4 * 512:(b4 + 1) * 512],
                         start=(ic == 0), stop=(ic == 15),
                         inc=("TP" if (ic == 15 and b4 == 3) else None))
            t_q = L.val("TP")
            if lc == NCH - 1:
                t_q_last = t_q
            L.wait("scalar", "TP", t_q)
            L.wait("scalar", "VP", prev["v_r"])
            for b4 in range(4):
                L.op("scalar", "activation", SCq[:, b4 * 512:(b4 + 1) * 512],
                     PB[SP_[b4]][:], AF.Relu, scale=INV_SQRT_DH,
                     inc=("SP" if b4 == 3 else None))
            s_q = L.val("SP")
            L.wait("tensor", "SP", s_q)
            for ic in range(16):
                L.op("tensor", "transpose",
                     PBb(SP_[ic // 8])[:, (ic % 8) * 128:(ic % 8 + 1) * 128],
                     SCq[:, ic * 128:(ic + 1) * 128], idbf[:],
                     inc=("TP" if ic == 15 else None))
            t_qt = L.val("TP")
            L.wait("vector", "TP", t_qt)
            for bank in range(2):
                L.op("vector", "tensor_copy",
                     SCqfm[:, bank * 1024:(bank + 1) * 1024],
                     PBb(SP_[bank])[:, 0:1024],
                     inc=("VP" if bank == 1 else None))
            v_qt = L.val("VP")
            def emit_o_block():
                L.wait("tensor", "VP", v_qt)
                if lc == 1:
                    L.wait("tensor", "VP", prev["v_on"])
                    L.wait("tensor", "SP", prev["s_ss"])
                else:
                    L.wait("tensor", "SP", prev["s_u"])
                L.wait("tensor", "VP", prev["r_evac"][1 - p])
                if lc == 0:
                    L.wait("tensor", "LW", lw_kv)
                for h in range(H):
                    L.op("tensor", "matmul",
                         PB[SO_[h // 4]][:, (h % 4) * 128:(h % 4 + 1) * 128],
                         SCqfm[:, h * 128:(h + 1) * 128],
                         kv_sb[:, h * 128:(h + 1) * 128],
                         start=(h % 4 == 0), stop=(h % 4 == 3),
                         skip_group_check=True,
                         inc=("TP" if h == H - 1 else None))
                t_o = L.val("TP")
                L.wait("scalar", "TP", t_o)
                for h in range(H):
                    L.op("scalar", "activation", SCq[:, 0:128],
                         PB[SO_[h // 4]][:, (h % 4) * 128:(h % 4 + 1) * 128],
                         AF.Square, accum_out=SL(9)[:, h:h + 1],
                         inc=("SP" if h == H - 1 else None))
                s_ss = L.val("SP")
                L.wait("vector", "SP", s_ss)
                L.op("vector", "tensor_scalar", SL(11), SL(9), 1.0 / DH, 1e-30,
                     ALU.mult, ALU.max)
                dv()
                L.op("vector", "reciprocal", SL(10), SL(11), inc="VP")
                v_sd = L.val("VP")
                L.wait("scalar", "VP", v_sd)
                L.op("scalar", "activation", SL(10), SL(10), AF.Sqrt, inc="SP")
                s_sd = L.val("SP")
                L.wait("vector", "SP", s_sd)
                L.op("vector", "tensor_tensor", SL(12), SL(10), SL(10), ALU.mult)
                dv()
                L.op("vector", "tensor_tensor", SL(12), SL(12), SL(11), ALU.mult)
                dv()
                L.op("vector", "tensor_scalar", SL(12), SL(12), -0.5, 1.5,
                     ALU.mult, ALU.add)
                dv()
                L.op("vector", "tensor_tensor", SL(10), SL(10), SL(12), ALU.mult)
                dv()
                L.op("vector", "tensor_scalar", SL(10), SL(10), 1e12, None,
                     ALU.min, inc="VP")
                v_f2 = L.val("VP")
                dv()
                return t_o, s_ss, v_f2

            def emit_u_block():
                L.wait("tensor", "VP", v_qt)
                for b4 in range(4):
                    if lc == 0:
                        L.wait("tensor", "LW", lw_wu4[b4])
                    for ic in range(16):
                        L.op("tensor", "matmul", PB[SP_[b4]][:],
                             YFv[:, ic, lc * 128:(lc + 1) * 128],
                             RAv[:, ic, b4 * 512:(b4 + 1) * 512],
                             start=(ic == 0), stop=(ic == 15),
                             inc=("TP" if (ic == 15 and b4 == 3) else None))
                t_u2 = L.val("TP")
                L.wait("scalar", "TP", t_u2)
                ustrip2 = YFv[:, :, lc * 128:(lc + 1) * 128]
                for b in range(4):
                    L.op("scalar", "activation",
                         ustrip2[:, b * 4:(b + 1) * 4, :],
                         PB[SP_[b]][:].rearrange("p (a c) -> p a c", c=128),
                         AF.Silu, inc=("SP" if b == 3 else None))
                return t_u2, L.val("SP"), ustrip2

            def do_flush_rt():
                if pend_rt:
                    set_flushed = pend_rt[0][1]
                    r_evac_v = flush_rt()
                    prev["r_evac"][set_flushed] = r_evac_v

            if lc <= 1:
                t_u, s_u, ustrip = emit_u_block()
                do_flush_rt()
                t_o, s_ss, v_f = emit_o_block()
            else:
                do_flush_rt()
                t_o, s_ss, v_f = emit_o_block()
                t_u, s_u, ustrip = emit_u_block()
            L.wait("vector", "VP", v_f)
            for b in range(4):
                fb = SL(10)[:, b * 4:(b + 1) * 4].unsqueeze(2).broadcast_to(
                    [128, 4, 128])
                L.op("vector", "tensor_tensor",
                     SCq[:, b * 512:(b + 1) * 512].rearrange(
                         "p (a c) -> p a c", c=128),
                     PB[SO_[b]][:].rearrange("p (a c) -> p a c", c=128),
                     fb, ALU.mult, inc=("VP" if b == 3 else None))
            v_on = L.val("VP")
            L.wait("vector", "SP", s_u)
            L.wait("vector", "SP", s_u)
            L.op("vector", "tensor_tensor",
                 SCqfm[:, :].rearrange("p (a c) -> p a c", c=128),
                 SCq[:, :].rearrange("p (a c) -> p a c", c=128),
                 ustrip, ALU.mult, inc="VP")
            v_r = L.val("VP")
            pend_rt.append((lc, RTSET[lc], v_r))
            prev = {"v_on": v_on, "s_ss": s_ss, "s_u": s_u, "v_r": v_r,
                    "r_evac": prev["r_evac"], "t_rt": prev["t_rt"]}
        while pend_rt:
            flush_rt()
        v_b34 = L.val("VP")

        lw_wo4 = []
        for qtr in range(4):
            dst = RBv[:, :, qtr * 512:(qtr + 1) * 512]
            lw_wo4.append(load_w(dst, wod, qtr * 512, 512,
                                 "TP" if qtr == 0 else None, t_q_last))

        L.barrier()
        b5_xin_rel = [0, 0]
        b5_out_rel = [0, 0]
        b5_tb = {b: 0 for b in range(8)}
        b5_marks = []
        b34_last_tp = L.val("TP")
        for lc in range(NCH):
            sl = lc % 2
            if lc < 2:
                L.wait("sync", "TP", b34_last_tp)
            L.wait("sync", "VP", b5_xin_rel[sl])
            L.op("sync", "dma_start", out=B5_xin(sl),
                 in_=x1b[lc * 128:(lc + 1) * 128, :], inc="LX", incn=16)
            ld = L.val("LX")
            banks = [(lc * 4 + ot) % 8 for ot in range(4)]
            for b in banks:
                L.wait("tensor", "VP", b5_tb[b])
            for ot in range(4):
                if lc == 0:
                    L.wait("tensor", "LW", lw_wo4[ot])
                for ic in range(16):
                    L.op("tensor", "matmul", PB[banks[ot]][:],
                         YFv[:, ic, lc * 128:(lc + 1) * 128],
                         RBv[:, ic, ot * 512:(ot + 1) * 512],
                         start=(ic == 0), stop=(ic == 15),
                         inc=("TP" if (ic == 15 and ot == 3) else None))
            t_mm = L.val("TP")
            L.wait("vector", "TP", t_mm)
            L.wait("vector", "LX", ld)
            L.wait("vector", "SO", b5_out_rel[sl])
            for ot in range(4):
                L.op("vector", "tensor_tensor",
                     B5_out(sl)[:, ot * 512:(ot + 1) * 512], PB[banks[ot]][:],
                     B5_xin(sl)[:, ot * 512:(ot + 1) * 512], ALU.add, inc="VP")
                b5_tb[banks[ot]] = L.val("VP")
            v_x2 = L.val("VP")
            b5_xin_rel[sl] = v_x2
            L.op("vector", "tensor_reduce", SL(0)[:, lc:lc + 1], B5_out(sl),
                 AX.X, ALU.add)
            L.op("vector", "tensor_reduce", SL(2)[:, lc:lc + 1], B5_out(sl),
                 AX.X, ALU.max)
            L.op("vector", "tensor_reduce", SL(3)[:, lc:lc + 1], B5_out(sl),
                 AX.X, ALU.min, inc="VP")
            v_st = L.val("VP")
            L.wait("scalar", "VP", v_st)
            L.op("scalar", "activation", junk_b5[:, :], B5_out(sl), AF.Square,
                 accum_out=SL(1)[:, lc:lc + 1], inc="SP")
            s_st = L.val("SP")
            L.wait("scalar", "SP", s_st)
            L.op("scalar", "dma_start", out=x2b[lc * 128:(lc + 1) * 128, :],
                 in_=B5_out(sl), inc="SO", incn=16)
            b5_out_rel[sl] = L.val("SO")
            b5_marks.append(b5_out_rel[sl])
            L.wait("gpsimd", "TP", t_mm)
            src = wtsp.rearrange("p (a b) -> p a b", b=D)[:, :, lc * 128:(lc + 1) * 128]
            L.op("gpsimd", "dma_start", out=WT8_C[:, :, lc * 128:(lc + 1) * 128],
                 in_=src, inc="LW", incn=16)
        lw_wt_reload = L.val("LW")
        b5_last_tp = L.val("TP")

        if STAGE == "B":
            L.wait("gpsimd", "SO", L.val("SO"))
            L.op("gpsimd", "dma_start", out=outd, in_=x2b, inc="SO", incn=16)
            final_so = L.val("SO")
        else:
            L.barrier()
            marks_c = batch_ln_finalize(True, mx3i, mx3o)
            L.barrier()
            L.wait("tensor", "LW", lw_wt_reload)
            L.wait("sync", "TP", b5_last_tp)
            L.op("sync", "dma_start", out=C_out(0)[0:1, 0:2048], in_=bd,
                 inc="LX", incn=16)
            ld_bias_c = L.val("LX")
            L.wait("vector", "LX", ld_bias_c)
            L.op("vector", "tensor_scalar", biasrow, C_out(0)[0:1, 0:2048], 0.5,
                 None, ALU.mult, inc="VP")
            v_biasrow_c = L.val("VP")
            pipe["tr"] = L.val("TP")
            pipe["trevac"] = ("VP", marks_c["psum0_free"])
            pipe["y2t"] = 0
            pipe["y2evac"] = ("VP", 0)
            c_guards = [[("TP", b5_last_tp)], [("TP", b5_last_tp)]]
            ffn_phase(x2b, outd, WT8_C, C_xin, C_out, False, marks_c["rstd_s"],
                      b5_marks, v_biasrow_c, c_guards)
            final_so = L.val("SO")

    L.wait("gpsimd", "SO", final_so)
    if DEBUG:
        dstt = nc.dram_tensor("dstt", [128, 13 * 16], F32, kind="ExternalOutput").ap()
        dgmax = nc.dram_tensor("dgmax", [128, 1], F32, kind="ExternalOutput").ap()
        dws = nc.dram_tensor("dws", [128, 1], F32, kind="ExternalOutput").ap()
        dscr = nc.dram_tensor("dscr", [128, 7168], F32, kind="ExternalOutput").ap()
        dra = nc.dram_tensor("dra", [128, 16 * D], F32, kind="ExternalOutput").ap()
        L.op("gpsimd", "dma_start", out=dstt, in_=stt[:], inc="SO", incn=16)
        L.op("gpsimd", "dma_start", out=dgmax, in_=gmax_bc[:], inc="SO", incn=16)
        L.op("gpsimd", "dma_start", out=dws, in_=ws_bc[:], inc="SO", incn=16)
        L.op("gpsimd", "dma_start", out=dscr, in_=SCR[:], inc="SO", incn=16)
        L.op("gpsimd", "dma_start", out=dra, in_=RA[:], inc="SO", incn=16)
        L.wait("gpsimd", "SO", L.val("SO"))
    L.op("gpsimd", "nop", inc="GP")

    sems = {}
    for name in L.sem_names:
        sems[name] = ctx.enter_context(nc.semaphore(name))
    block = ctx.enter_context(nc.Block())

    def runner(eng_name):
        def run(e):
            for fn in L.ops[eng_name]:
                fn(e, sems)
        return run

    for eng_name in ENGINES:
        getattr(block, eng_name)(runner(eng_name))

    ctx.close()
    n_instr = sum(len(v) for v in L.ops.values())
    return nc, n_instr


def kernel(L0_attn_x, L0_attn_Wq, L0_attn_Wk, L0_attn_Wv, L0_attn_Wu,
           L0_attn_Wo, L0_ffn_weight, L0_ffn_weight_scale, L0_ffn_bias):
    x = np.ascontiguousarray(
        np.asarray(L0_attn_x, dtype=np.float32).reshape(16384, 2048))
    shards = [np.ascontiguousarray(x[c * T:(c + 1) * T]) for c in range(NCORES)]
    wq = np.ascontiguousarray(np.asarray(L0_attn_Wq, dtype=np.float32))
    wk = np.ascontiguousarray(np.asarray(L0_attn_Wk, dtype=np.float32))
    wv = np.ascontiguousarray(np.asarray(L0_attn_Wv, dtype=np.float32))
    wu = np.ascontiguousarray(np.asarray(L0_attn_Wu, dtype=np.float32))
    wo = np.ascontiguousarray(np.asarray(L0_attn_Wo, dtype=np.float32))
    wf = np.ascontiguousarray(np.asarray(L0_ffn_weight, dtype=np.float32))
    ws = np.ascontiguousarray(np.asarray(L0_ffn_weight_scale, dtype=np.float32).reshape(1, 1))
    bias = np.ascontiguousarray(np.asarray(L0_ffn_bias, dtype=np.float32).reshape(1, 2048))
    idbf = np.eye(128, dtype=ml_dtypes.bfloat16)
    idf8 = np.eye(128, dtype=ml_dtypes.float8_e4m3)
    onesb = np.ones((1, 128), dtype=ml_dtypes.bfloat16)

    nc, _ = build_program()
    in_maps = [{
        "x0": shards[c], "wq": wq, "wk": wk, "wv": wv, "wu": wu, "wo": wo,
        "wf": wf, "ws": ws, "bias": bias, "idbf": idbf, "idf8": idf8,
        "onesb": onesb,
    } for c in range(NCORES)]
    res = run_bass_kernel_spmd(nc, in_maps, core_ids=list(range(NCORES)))
    out = np.concatenate([res.results[c]["out"] for c in range(NCORES)], axis=0)
    return out.reshape(1, 16384, 2048)


# version 18 (final)
# speedup vs baseline: 1.0199x; 1.0199x over previous
import numpy as np
import ml_dtypes
import concourse.bass as bass
import concourse.mybir as mybir
from concourse.bass_utils import run_bass_kernel_spmd
from contextlib import ExitStack

F32 = mybir.dt.float32
BF16 = mybir.dt.bfloat16
FP8 = mybir.dt.float8e4
AF = mybir.ActivationFunctionType
ALU = mybir.AluOpType
AX = mybir.AxisListType
PMDR = mybir.MatmulPerfMode.DoubleRow

NCORES = 8
T = 2048
D = 2048
NCH = 16
H = 16
DH = 128
INV_SQRT_DH = float(1.0 / np.sqrt(np.float32(128.0)))

ENGINES = ("sync", "vector", "scalar", "tensor", "gpsimd")

STAGE = "full"
DEBUG = False


class Ledger:
    def __init__(self):
        self.ops = {e: [] for e in ENGINES}
        self.sem_names = []
        self.counts = {}

    def sem(self, name):
        if name not in self.counts:
            self.sem_names.append(name)
            self.counts[name] = 0
        return name

    def inc(self, name, n=1):
        self.sem(name)
        self.counts[name] += n
        return self.counts[name]

    def val(self, name):
        return self.counts.get(name, 0)

    def emit(self, eng, fn):
        self.ops[eng].append(fn)

    def wait(self, eng, name, v):
        if v is None or v <= 0:
            return
        self.sem(name)
        self.emit(eng, lambda e, s, nm=name, vv=v: e.wait_ge(s[nm], vv))

    def op(self, eng, meth, *args, inc=None, incn=1, **kw):
        v = None
        if inc is not None:
            v = self.inc(inc, incn)

        def fn(e, s, meth=meth, args=args, kw=kw, inc=inc, incn=incn):
            r = getattr(e, meth)(*args, **kw)
            if inc is not None:
                r.then_inc(s[inc], incn)
        self.emit(eng, fn)
        return v

    def barrier(self, engines=("tensor", "vector", "scalar")):
        self.sem("BAR")
        tgt = self.val("BAR") + len(engines)
        for eng in engines:
            self.inc("BAR")
            self.emit(eng, lambda e, s: e.drain().then_inc(s["BAR"], 1))
        for eng in engines:
            self.wait(eng, "BAR", tgt)


def build_program():
    nc = bass.Bass("TRN2", target_bir_lowering=False)

    x0d = nc.dram_tensor("x0", [T, D], F32, kind="ExternalInput").ap()
    wqd = nc.dram_tensor("wq", [D, D], F32, kind="ExternalInput").ap()
    wkd = nc.dram_tensor("wk", [D, D], F32, kind="ExternalInput").ap()
    wvd = nc.dram_tensor("wv", [D, D], F32, kind="ExternalInput").ap()
    wud = nc.dram_tensor("wu", [D, D], F32, kind="ExternalInput").ap()
    wod = nc.dram_tensor("wo", [D, D], F32, kind="ExternalInput").ap()
    wfd = nc.dram_tensor("wf", [D, D], F32, kind="ExternalInput").ap()
    wsd = nc.dram_tensor("ws", [1, 1], F32, kind="ExternalInput").ap()
    bd = nc.dram_tensor("bias", [1, D], F32, kind="ExternalInput").ap()
    idbf_d = nc.dram_tensor("idbf", [128, 128], BF16, kind="ExternalInput").ap()
    idf8_d = nc.dram_tensor("idf8", [128, 128], FP8, kind="ExternalInput").ap()
    onesb_d = nc.dram_tensor("onesb", [1, 128], BF16, kind="ExternalInput").ap()
    outd = nc.dram_tensor("out", [T, D], F32, kind="ExternalOutput").ap()

    x1b = nc.dram_tensor("x1buf", [T, D], F32).ap()
    x2b = nc.dram_tensor("x2buf", [T, D], F32).ap()
    wtsp = nc.dram_tensor("wtspill", [128, 16 * D], FP8).ap()
    mx1i = nc.dram_tensor("mx1i", [1, 1], F32).ap()
    mx1o = nc.dram_tensor("mx1o", [1, 1], F32).ap()
    mx3i = nc.dram_tensor("mx3i", [1, 1], F32).ap()
    mx3o = nc.dram_tensor("mx3o", [1, 1], F32).ap()
    mxstg = nc.dram_tensor("mxstg", [1, 128], F32).ap()
    kvi = nc.dram_tensor("kvi", [128, H * DH], F32).ap()
    kvo = nc.dram_tensor("kvo", [128, H * DH], F32).ap()

    L = Ledger()
    ctx = ExitStack()

    def dv():
        L.op("vector", "drain")

    YF = ctx.enter_context(nc.sbuf_tensor("YF", [128, 16 * D], BF16))
    RA = ctx.enter_context(nc.sbuf_tensor("RA", [128, 16 * D], BF16))
    RB = ctx.enter_context(nc.sbuf_tensor("RB", [128, 16 * D], BF16))
    SCR = ctx.enter_context(nc.sbuf_tensor("SCR", [128, 7168], BF16))
    idbf = ctx.enter_context(nc.sbuf_tensor("idbf_s", [128, 128], BF16))
    idf8 = ctx.enter_context(nc.sbuf_tensor("idf8_s", [128, 128], FP8))
    onesb = ctx.enter_context(nc.sbuf_tensor("onesb_s", [1, 128], BF16))
    ws_bc = ctx.enter_context(nc.sbuf_tensor("ws_bc", [128, 1], F32))
    gmax_bc = ctx.enter_context(nc.sbuf_tensor("gmax_bc", [128, 1], F32))
    stt = ctx.enter_context(nc.sbuf_tensor("stt", [128, 13 * 16], F32))
    mxv = ctx.enter_context(nc.sbuf_tensor("mxv", [128, 3], F32))
    mx11 = ctx.enter_context(nc.sbuf_tensor("mx11", [1, 2], F32))

    def SL(idx):
        return stt[:, idx * 16:(idx + 1) * 16]

    PB = [ctx.enter_context(nc.psum_tensor(f"PB{i}", [128, 512], F32))
          for i in range(8)]

    def PBb(i):
        return PB[i][:].bitcast(BF16)

    RAv = RA[:].rearrange("p (a b) -> p a b", b=D)
    RA8 = RA[:].bitcast(FP8)
    RA8u = RA8[:, 32768:65536].rearrange("p (a b) -> p a b", b=D)
    RB8 = RB[:].bitcast(FP8)
    Wnat8 = RB8[:, 32768:65536].rearrange("p (a b) -> p a b", b=D)
    YF8 = YF[:].bitcast(FP8)
    YF8v = YF8[:].rearrange("p (a b) -> p a b", b=2 * D)
    WT8_C = YF8v[:, :, 0:D]
    RBv = RB[:].rearrange("p (a b) -> p a b", b=D)
    YFv = YF[:].rearrange("p (a b) -> p a b", b=D)
    RAh = RA[:].rearrange("p (a b) -> p a b", b=1024)
    RBh = RB[:].rearrange("p (a b) -> p a b", b=1024)
    YF32 = YF[:].bitcast(F32)
    RA32 = RA[:].bitcast(F32)
    RB32 = RB[:].bitcast(F32)

    def A1_xin(sl):
        return YF32[:, sl * 2048:(sl + 1) * 2048]
    bias_stage = YF32[0:1, 8192:10240]

    def A_xin(sl):
        return RA32[:, sl * 2048:(sl + 1) * 2048]

    def A_out(sl):
        return RA32[:, (2 + sl) * 2048:(3 + sl) * 2048]

    def B5_xin(sl):
        return RA32[:, sl * 2048:(sl + 1) * 2048]

    def B5_out(sl):
        return RA32[:, (2 + sl) * 2048:(3 + sl) * 2048]

    def C_xin(sl):
        return RB32[:, sl * 2048:(sl + 1) * 2048]

    def C_out(sl):
        return RB32[:, (2 + sl) * 2048:(3 + sl) * 2048]

    q8 = SCR[:, 0:1024].bitcast(FP8)
    xqfm8s = [SCR[:, 1024:2048].bitcast(FP8),
              SCR[:, 4096:5120].bitcast(FP8)]
    y_tm2 = SCR[:, 2048:4096]
    biasrow = SCR[0:1, 5120:7168]
    k_sbs = [SCR[:, 0:1024], SCR[:, 2048:3072]]
    v_sbs = [SCR[:, 1024:2048], SCR[:, 3072:4096]]
    kvstage = SCR[:, 4096:6144].bitcast(F32)
    SCq = SCR[:, 0:2048]
    SCqfm = SCR[:, 2048:4096]
    junk_b5 = SCR[:, 0:2048]
    kv_sb = SCR[:, 4096:6144]

    for s in ("LX", "LW", "LR", "SO", "TP", "VP", "SP", "GP", "CC", "BAR"):
        L.sem(s)

    L.op("sync", "dma_start", out=idbf[:], in_=idbf_d, inc="LX", incn=16)
    L.op("sync", "dma_start", out=onesb[:], in_=onesb_d, inc="LX", incn=16)
    L.op("sync", "dma_start", out=mx11[0:1, 1:2], in_=wsd[0:1, 0:1],
         inc="LX", incn=16)
    L.op("sync", "dma_start", out=idf8[:], in_=idf8_d, inc="LX", incn=16)
    L.op("sync", "dma_start", out=bias_stage, in_=bd, inc="LX", incn=16)
    consts_ld = L.val("LX")

    wf_view = wfd.rearrange("(a p) i -> p a i", p=128)
    wffn_ld4 = []
    for q4 in range(4):
        L.op("gpsimd", "dma_start", out=Wnat8[:, q4 * 4:(q4 + 1) * 4, :],
             in_=wf_view[:, q4 * 4:(q4 + 1) * 4, :], inc="LW", incn=16)
        wffn_ld4.append(L.val("LW"))

    L.wait("vector", "LX", consts_ld)
    h11 = mx11[0:1, 0:1]
    ws11 = mx11[0:1, 1:2]
    L.op("vector", "tensor_tensor", h11, ws11, ws11, ALU.mult)
    dv()
    L.op("vector", "tensor_scalar", h11, h11, 1e-30, None, ALU.max)
    dv()
    L.op("vector", "reciprocal", h11, h11)
    dv()
    L.op("vector", "tensor_tensor", h11, h11, ws11, ALU.mult)
    dv()
    L.op("vector", "tensor_scalar", h11, h11, 0.5, None, ALU.mult)
    dv()
    L.op("vector", "tensor_scalar", biasrow, bias_stage, h11, None,
         ALU.mult, inc="VP")
    v_biasrow = L.val("VP")

    ones32 = SCR[:, 0:256].bitcast(F32)[0:1, 0:128]
    g11 = SCR[:, 256:260].bitcast(F32)[0:1, 0:1]

    def bcast_128(src_11, dst_128, wait_list, who_waits_next):
        for (eng, sem, vv) in wait_list:
            L.wait(eng, sem, vv)
        L.op("vector", "memset", ones32, 1.0)
        dv()
        L.op("vector", "nop", inc="VP")
        vm = L.val("VP")
        L.wait("tensor", "VP", vm)
        L.op("tensor", "matmul", PB[0][:, 0:1], ones32, src_11,
             start=True, stop=True, inc="TP")
        tm = L.val("TP")
        L.wait("vector", "TP", tm)
        L.op("vector", "tensor_copy", dst_128, PB[0][:, 0:1], inc="VP")
        vv = L.val("VP")
        dv()
        return vv

    v_wsbc = bcast_128(mx11[0:1, 1:2], ws_bc[:], [("vector", "LX", consts_ld)],
                       "scalar")

    L.wait("tensor", "LX", consts_ld)
    L.wait("tensor", "VP", v_wsbc)
    L.wait("scalar", "VP", v_wsbc)
    def PBf8(i):
        return PB[i][:].bitcast(FP8)
    s_wt_evac = 0
    for oc in range(16):
        if oc % 4 == 0:
            L.wait("tensor", "LW", wffn_ld4[oc // 4])
        L.wait("tensor", "SP", s_wt_evac)
        for ic in range(16):
            b = ic // 8
            sl8 = ic % 8
            L.op("tensor", "transpose",
                 PBf8(b)[:, sl8 * 256:(sl8 + 1) * 256:2],
                 Wnat8[:, oc, ic * 128:(ic + 1) * 128], idf8[:],
                 inc=("TP" if ic == 15 else None))
        t_wt = L.val("TP")
        L.wait("scalar", "TP", t_wt)
        for bank in range(2):
            dst = RA8u[:, bank * 8:(bank + 1) * 8, oc * 128:(oc + 1) * 128]
            src = PBf8(bank)[:, 0:2048].rearrange(
                "p (t c) -> p t c", c=256)[:, :, 0:256:2]
            L.op("scalar", "copy", dst, src,
                 inc=("SP" if bank == 1 else None))
        s_wt_evac = L.val("SP")
    t_wt_done = L.val("TP")
    s_wt_done = L.val("SP")

    L.wait("gpsimd", "SP", s_wt_done)
    L.op("gpsimd", "dma_start", out=wtsp, in_=RA8[:, 32768:65536], inc="LW", incn=16)

    xin_rel = [(0, 0), (0, 0)]
    for lc in range(NCH):
        sl = lc % 2
        L.wait("sync", "VP", xin_rel[sl][0])
        L.wait("sync", "SP", xin_rel[sl][1])
        L.op("sync", "dma_start", out=A1_xin(sl),
             in_=x0d[lc * 128:(lc + 1) * 128, :], inc="LX", incn=16)
        ld = L.val("LX")
        L.wait("vector", "LX", ld)
        L.op("vector", "tensor_reduce", SL(2)[:, lc:lc + 1], A1_xin(sl), AX.X, ALU.max)
        L.op("vector", "tensor_reduce", SL(3)[:, lc:lc + 1], A1_xin(sl), AX.X, ALU.min,
             inc="VP")
        v_done = L.val("VP")
        L.wait("scalar", "LX", ld)
        L.op("scalar", "activation", y_tm2[:, 0:2048], A1_xin(sl), AF.Square,
             accum_out=SL(1)[:, lc:lc + 1])
        L.op("scalar", "activation", y_tm2[:, 0:2048], A1_xin(sl), AF.Copy,
             accum_out=SL(0)[:, lc:lc + 1], inc="SP")
        s_done = L.val("SP")
        xin_rel[sl] = (v_done, s_done)

    def batch_ln_finalize(with_max, cc_in, cc_out):
        L.op("vector", "tensor_scalar", SL(4), SL(0), -1.0 / D, None, ALU.mult)
        dv()
        L.op("vector", "tensor_tensor", SL(7), SL(4), SL(4), ALU.mult)
        dv()
        L.op("vector", "scalar_tensor_tensor", SL(8), SL(1), 1.0 / D, SL(7),
             ALU.mult, ALU.subtract)
        dv()
        L.op("vector", "tensor_scalar", SL(8), SL(8), 1e-5, None, ALU.add)
        dv()
        L.op("vector", "reciprocal", SL(5), SL(8), inc="VP")
        v_seed = L.val("VP")
        L.wait("scalar", "VP", v_seed)
        L.op("scalar", "activation", SL(5), SL(5), AF.Sqrt, inc="SP")
        s_seed = L.val("SP")
        L.wait("vector", "SP", s_seed)
        for it in range(2):
            L.op("vector", "tensor_tensor", SL(7), SL(5), SL(5), ALU.mult)
            dv()
            L.op("vector", "tensor_tensor", SL(7), SL(7), SL(8), ALU.mult)
            dv()
            L.op("vector", "tensor_scalar", SL(7), SL(7), -0.5, 1.5, ALU.mult, ALU.add)
            dv()
            L.op("vector", "tensor_tensor", SL(5), SL(5), SL(7), ALU.mult,
                 inc=("VP" if it == 1 else None))
            dv()
        marks = {}
        marks["rstd"] = L.val("VP")
        marks["psum0_free"] = 0
        if with_max:
            L.op("vector", "tensor_tensor", SL(7), SL(2), SL(4), ALU.add)
            L.op("vector", "tensor_tensor", SL(8), SL(3), SL(4), ALU.add)
            dv()
            L.op("vector", "tensor_scalar", SL(8), SL(8), -1.0, None, ALU.mult)
            dv()
            L.op("vector", "tensor_tensor", SL(7), SL(7), SL(8), ALU.max)
            dv()
            L.op("vector", "tensor_tensor", SL(7), SL(7), SL(5), ALU.mult)
            dv()
            L.op("vector", "tensor_reduce", mxv[:, 0:1], SL(7), AX.X, ALU.max,
                 inc="VP")
            v_mx = L.val("VP")
            L.wait("sync", "VP", v_mx)
            L.op("sync", "dma_start", out=mxstg, in_=mxv[:, 0:1],
                 inc="LX", incn=16)
            g_a = L.val("LX")
            L.wait("sync", "LX", g_a)
            mxrow = SCR[:, 512:768].bitcast(F32)[0:1, 0:128]
            L.op("sync", "dma_start", out=mxrow, in_=mxstg, inc="LX", incn=16)
            g_b = L.val("LX")
            L.wait("vector", "LX", g_b)
            L.op("vector", "tensor_reduce", mx11[0:1, 0:1], mxrow,
                 AX.X, ALU.max, inc="VP")
            v_mx11 = L.val("VP")
            L.wait("sync", "VP", v_mx11)
            L.op("sync", "dma_start", out=cc_in, in_=mx11[0:1, 0:1],
                 inc="LX", incn=16)
            g0 = L.val("LX")
            L.wait("gpsimd", "LX", g0)
            L.op("gpsimd", "collective_compute", "AllReduce", ALU.max,
                 replica_groups=[list(range(NCORES))],
                 ins=[cc_in.opt()], outs=[cc_out.opt()], inc="CC")
            cc_v = L.val("CC")
            L.wait("gpsimd", "CC", cc_v)
            L.op("gpsimd", "dma_start", out=g11, in_=cc_out[0:1, 0:1],
                 inc="GP", incn=16)
            g1 = L.val("GP")
            v_gbc = bcast_128(g11, gmax_bc[:], [("vector", "GP", g1)], "vector")
            marks["psum0_free"] = v_gbc
            L.op("vector", "tensor_scalar", mxv[:, 1:2], gmax_bc[:], 1.0 / 448.0,
                 1e-7, ALU.mult, ALU.max)
            dv()
            dv()
            L.op("vector", "tensor_scalar", mxv[:, 1:2], mxv[:, 1:2], 2.0, None,
                 ALU.mult)
            dv()
            L.op("vector", "reciprocal", mxv[:, 2:3], mxv[:, 1:2])
            dv()
            L.op("vector", "tensor_scalar", SL(6), SL(5), mxv[:, 2:3], None,
                 ALU.mult, inc="VP")
            marks["rstd_s"] = L.val("VP")
        dv()
        return marks

    L.barrier()
    marks_a = batch_ln_finalize(True, mx1i, mx1o)

    pipe = {"tr": t_wt_done, "trevac": ("VP", marks_a["psum0_free"]),
            "y2t": 0, "y2evac": ("VP", 0)}

    def ffn_phase(xsrc_d, xdst_d, wt8v, xin_f, xout_f, fuse_ln2, rstd_s_mark,
                  src_store_marks, v_biasrow_mark, xin_guards=None):
        xin_rel = [0, 0]
        tb_free = {b: 0 for b in range(2, 6)}
        store_marks = [None] * NCH
        sq_marks = [0] * NCH
        pend_resid = []
        pend_post = []
        y2_pend = []
        L.wait("tensor", "VP", v_biasrow_mark)

        def load_chunk(lc):
            sl = lc % 2
            if lc < 2 and xin_guards is not None:
                for (gs, gv) in xin_guards[sl]:
                    L.wait("sync", gs, gv)
            L.wait("sync", "VP", xin_rel[sl])
            if src_store_marks is not None:
                L.wait("sync", "SO", src_store_marks[lc])
            L.op("sync", "dma_start", out=xin_f(sl),
                 in_=xsrc_d[lc * 128:(lc + 1) * 128, :], inc="LX", incn=16)
            return L.val("LX")

        def quantize(lc, ld):
            sl = lc % 2
            L.wait("vector", "LX", ld)
            if lc == 0:
                L.wait("vector", "VP", rstd_s_mark)
            L.wait("vector", "TP", pipe["tr"])
            L.op("vector", "tensor_scalar", q8[:, :], xin_f(sl),
                 SL(4)[:, lc:lc + 1], SL(6)[:, lc:lc + 1], ALU.add, ALU.mult,
                 inc="VP")
            return L.val("VP")

        def xq_transpose(lc, v_q8):
            L.wait("tensor", "VP", v_q8)
            L.wait("tensor", pipe["trevac"][0], pipe["trevac"][1])
            for ic in range(16):
                b = ic // 8
                sl8 = ic % 8
                L.op("tensor", "transpose",
                     PBf8(b)[:, sl8 * 256:(sl8 + 1) * 256:2],
                     q8[:, ic * 128:(ic + 1) * 128], idf8[:],
                     inc=("TP" if ic == 15 else None))
            pipe["tr"] = L.val("TP")
            L.wait("scalar", "TP", pipe["tr"])
            xq = xqfm8s[lc % 2]
            for bank in range(2):
                srcap = PBf8(bank)[:, 0:2048].rearrange(
                    "p (t c) -> p t c", c=256)[:, :, 0:256:2]
                dstap = xq[:, bank * 1024:(bank + 1) * 1024].rearrange(
                    "p (t c) -> p t c", c=128)
                L.op("scalar", "copy", dstap, srcap,
                     inc=("SP" if bank == 1 else None))
            pipe["trevac"] = ("SP", L.val("SP"))

        def flush_post():
            lc0, sl0, v0 = pend_post.pop(0)
            L.wait("scalar", "VP", v0)
            if fuse_ln2:
                L.wait("scalar", "TP", pipe["y2t"])
                L.op("scalar", "activation", y_tm2[:, :], xout_f(sl0), AF.Copy,
                     accum_out=SL(0)[:, lc0:lc0 + 1])
                L.op("scalar", "activation", y_tm2[:, :], xout_f(sl0), AF.Square,
                     accum_out=SL(1)[:, lc0:lc0 + 1], inc="SP")
            sq_marks[lc0] = L.val("SP")
            L.op("scalar", "dma_start", out=xdst_d[lc0 * 128:(lc0 + 1) * 128, :],
                 in_=xout_f(sl0), inc="SO", incn=16)
            store_marks[lc0] = L.val("SO")
            if not fuse_ln2:
                return None, None
            c = slice(lc0, lc0 + 1)
            L.wait("vector", "SP", sq_marks[lc0])
            L.op("vector", "tensor_scalar", SL(2)[:, c], SL(0)[:, c], -1.0 / D,
                 None, ALU.mult)
            dv()
            L.op("vector", "tensor_tensor", SL(7)[:, c], SL(2)[:, c], SL(2)[:, c],
                 ALU.mult)
            dv()
            L.op("vector", "scalar_tensor_tensor", SL(8)[:, c], SL(1)[:, c],
                 1.0 / D, SL(7)[:, c], ALU.mult, ALU.subtract)
            dv()
            L.op("vector", "tensor_scalar", SL(8)[:, c], SL(8)[:, c], 1e-5, None,
                 ALU.add)
            dv()
            L.op("vector", "reciprocal", SL(3)[:, c], SL(8)[:, c], inc="VP")
            v_seed = L.val("VP")
            L.wait("scalar", "VP", v_seed)
            L.op("scalar", "activation", SL(3)[:, c], SL(3)[:, c], AF.Sqrt,
                 inc="SP")
            s_seed = L.val("SP")
            L.wait("vector", "SP", s_seed)
            L.op("vector", "tensor_tensor", SL(7)[:, c], SL(3)[:, c],
                 SL(3)[:, c], ALU.mult)
            dv()
            L.op("vector", "tensor_tensor", SL(7)[:, c], SL(7)[:, c],
                 SL(8)[:, c], ALU.mult)
            dv()
            L.op("vector", "tensor_scalar", SL(7)[:, c], SL(7)[:, c], -0.5,
                 1.5, ALU.mult, ALU.add)
            dv()
            L.op("vector", "tensor_tensor", SL(3)[:, c], SL(3)[:, c],
                 SL(7)[:, c], ALU.mult)
            dv()
            L.op("vector", "tensor_tensor", SL(7)[:, c], SL(2)[:, c],
                 SL(3)[:, c], ALU.mult)
            dv()
            L.op("vector", "nop")
            L.op("vector", "engine_nop", inc="VP") if False else None
            L.op("vector", "tensor_copy", SL(8)[:, c], SL(7)[:, c], inc="VP")
            v_mb = L.val("VP")
            L.wait("scalar", "VP", v_mb)
            L.op("scalar", "activation", y_tm2[:, :], xout_f(sl0), AF.Identity,
                 scale=SL(3)[:, c], bias=SL(8)[:, c], inc="SP")
            s_aff = L.val("SP")
            return lc0, ("SP", s_aff)

        def flush_resid():
            lc0, sl0, banks0, t_mm0 = pend_resid.pop(0)
            L.wait("vector", "TP", t_mm0)
            if lc0 >= 2:
                L.wait("vector", "SO", store_marks[lc0 - 2])
                L.wait("vector", "SP", sq_marks[lc0 - 2])
            for ot in range(4):
                L.op("vector", "scalar_tensor_tensor",
                     xout_f(sl0)[:, ot * 512:(ot + 1) * 512], PB[banks0[ot]][:],
                     ws_bc[:], xin_f(sl0)[:, ot * 512:(ot + 1) * 512],
                     ALU.mult, ALU.add, inc="VP")
                tb_free[banks0[ot]] = L.val("VP")
            v_out = L.val("VP")
            xin_rel[sl0] = v_out
            pend_post.append((lc0, sl0, v_out))

        def emit_y2_transposes():
            lc0, v_aff = y2_pend.pop(0)
            L.wait("tensor", v_aff[0], v_aff[1])
            L.wait("tensor", pipe["y2evac"][0], pipe["y2evac"][1])
            for ic in range(16):
                b = 6 + ic // 8
                L.op("tensor", "transpose",
                     PBb(b)[:, (ic % 8) * 128:(ic % 8 + 1) * 128],
                     y_tm2[:, ic * 128:(ic + 1) * 128], idbf[:],
                     inc=("TP" if ic == 15 else None))
            t_y2 = L.val("TP")
            pipe["y2t"] = t_y2
            L.wait("scalar", "TP", t_y2)
            for bank in range(2):
                dst = YFv[:, bank * 8:(bank + 1) * 8, lc0 * 128:(lc0 + 1) * 128]
                src2 = PBb(6 + bank)[:, 0:1024].rearrange(
                    "p (t c) -> p t c", c=128)
                L.op("scalar", "copy", dst, src2,
                     inc=("SP" if bank == 1 else None))
            pipe["y2evac"] = ("SP", L.val("SP"))

        ld0 = load_chunk(0)
        vq = quantize(0, ld0)
        xq_transpose(0, vq)
        t_mm = 0
        for lc in range(NCH):
            if pend_resid:
                flush_resid()
            if lc + 1 < NCH:
                ld = load_chunk(lc + 1)
                vq = quantize(lc + 1, ld)
            if y2_pend:
                emit_y2_transposes()
            if len(pend_post) >= 2:
                lc0, v_aff = flush_post()
                if lc0 is not None:
                    y2_pend.append((lc0, v_aff))
            banks = [2 + (lc * 4 + ot) % 4 for ot in range(4)]
            L.wait("tensor", pipe["trevac"][0], pipe["trevac"][1])
            order = sorted(range(4), key=lambda ot: tb_free[banks[ot]])
            for ot in order:
                L.wait("tensor", "VP", tb_free[banks[ot]])
            xq = xqfm8s[lc % 2]
            for ot in order:
                for a in range(8):
                    stat = xq[:, (2 * a) * 128:(2 * a + 2) * 128].rearrange(
                        "p (j m) -> p j m", j=2)
                    mov = wt8v[:, 2 * a:2 * a + 2, ot * 512:(ot + 1) * 512]
                    L.op("tensor", "matmul", PB[banks[ot]][:], stat, mov,
                         start=(a == 0), stop=False, perf_mode=PMDR)
            for ot in range(4):
                L.op("tensor", "matmul", PB[banks[ot]][:], onesb[:],
                     biasrow[0:1, ot * 512:(ot + 1) * 512],
                     start=False, stop=True, inc=("TP" if ot == 3 else None))
            t_mm = L.val("TP")
            pend_resid.append((lc, lc % 2, banks, t_mm))
            if lc + 1 < NCH:
                xq_transpose(lc + 1, vq)
        while pend_resid or pend_post:
            while len(pend_post) >= 2 or (pend_post and not pend_resid):
                while y2_pend:
                    emit_y2_transposes()
                lc0, v_aff = flush_post()
                if lc0 is not None:
                    y2_pend.append((lc0, v_aff))
            if pend_resid:
                flush_resid()
        while y2_pend:
            emit_y2_transposes()
        return store_marks, t_mm, pipe["y2evac"]

    if STAGE == "A":
        ffn_phase(x0d, outd, RA8u, A_xin, A_out, False, marks_a["rstd_s"], None,
                  v_biasrow, None)
        final_so = L.val("SO")
    else:
        a2_marks, a2_last_mm, a2_tail_v = ffn_phase(
            x0d, x1b, RA8u, A_xin, A_out, True, marks_a["rstd_s"], None,
            v_biasrow, None)

        def load_w(dst_view, wd, col0, ncols, wait_sem=None, wait_v=0):
            if wait_sem is not None:
                L.wait("gpsimd", wait_sem, wait_v)
            src = wd[:, col0:col0 + ncols].rearrange("(a p) o -> p a o", p=128)
            L.op("gpsimd", "dma_start", out=dst_view, in_=src, inc="LW", incn=16)
            return L.val("LW")

        lw_wk1 = load_w(RBh[:, 0:16, :], wkd, 0, 1024, "TP", t_wt_done)
        lw_wv1 = load_w(RBh[:, 16:32, :], wvd, 0, 1024)
        L.wait("gpsimd", "SO", a2_marks[NCH - 1])
        L.wait("gpsimd", "TP", a2_last_mm)
        L.wait("gpsimd", a2_tail_v[0], a2_tail_v[1])
        lw_wk2 = load_w(RAh[:, 0:16, :], wkd, 1024, 1024)
        lw_wv2 = load_w(RAh[:, 16:32, :], wvd, 1024, 1024)

        if STAGE == "Y2":
            L.barrier()
            L.wait("gpsimd", "BAR", L.val("BAR"))
            L.op("gpsimd", "dma_start",
                 out=outd.rearrange("(a p) d -> p a d", p=128),
                 in_=YFv, inc="SO", incn=16)
            L.wait("gpsimd", "SO", L.val("SO"))
            sems = {}
            for name in L.sem_names:
                sems[name] = ctx.enter_context(nc.semaphore(name))
            block = ctx.enter_context(nc.Block())

            def runner2(eng_name):
                def run(e):
                    for fn in L.ops[eng_name]:
                        fn(e, sems)
                return run
            for eng_name in ENGINES:
                getattr(block, eng_name)(runner2(eng_name))
            ctx.close()
            return nc, sum(len(v) for v in L.ops.values())

        L.barrier()

        def b2_half(hbase, WH, lw_wait, pb67_free_v, kvstage_free_g):
            L.wait("tensor", "LW", lw_wait)
            L.wait("tensor", "VP", pb67_free_v)
            k_free = [L.val("SP")] * 2
            v_free = [L.val("VP")] * 2
            evac_marks = []

            def kv_mms(lck):
                pk = lck % 2
                sk, vv = evac_marks[lck]
                L.wait("tensor", "SP", sk)
                L.wait("tensor", "VP", vv)
                for hh in range(8):
                    L.op("tensor", "matmul",
                         PB[6 + hh // 4][:, (hh % 4) * 128:(hh % 4 + 1) * 128],
                         k_sbs[pk][:, hh * 128:(hh + 1) * 128],
                         v_sbs[pk][:, hh * 128:(hh + 1) * 128],
                         start=(lck == 0 and hh % 4 == 0), stop=(lck == NCH - 1),
                         skip_group_check=True,
                         inc=("TP" if hh == 7 else None))
                return L.val("TP")

            kv_done = {}
            for lc in range(NCH):
                pk = lc % 2
                L.wait("tensor", "SP", k_free[pk])
                for ic in range(16):
                    for ot in range(2):
                        L.op("tensor", "matmul", PB[ot][:],
                             YFv[:, ic, lc * 128:(lc + 1) * 128],
                             WH[:, ic, ot * 512:(ot + 1) * 512],
                             start=(ic == 0), stop=(ic == 15),
                             inc=("TP" if (ic == 15 and ot == 1) else None))
                t_k = L.val("TP")
                L.wait("tensor", "VP", v_free[pk])
                for ic in range(16):
                    for ot in range(2):
                        L.op("tensor", "matmul", PB[2 + ot][:],
                             YFv[:, ic, lc * 128:(lc + 1) * 128],
                             WH[:, 16 + ic, ot * 512:(ot + 1) * 512],
                             start=(ic == 0), stop=(ic == 15),
                             inc=("TP" if (ic == 15 and ot == 1) else None))
                t_v = L.val("TP")
                if lc >= 1:
                    kv_done[lc - 1] = kv_mms(lc - 1)
                L.wait("scalar", "TP", t_k)
                for ot in range(2):
                    L.op("scalar", "activation",
                         k_sbs[pk][:, ot * 512:(ot + 1) * 512],
                         PB[ot][:], AF.Relu, scale=INV_SQRT_DH,
                         inc=("SP" if ot == 1 else None))
                k_free[pk] = L.val("SP")
                L.wait("vector", "TP", t_v)
                for ot in range(2):
                    L.op("vector", "tensor_copy",
                         v_sbs[pk][:, ot * 512:(ot + 1) * 512],
                         PB[2 + ot][:], inc=("VP" if ot == 1 else None))
                v_free[pk] = L.val("VP")
                evac_marks.append((k_free[pk], v_free[pk]))
            kv_done[NCH - 1] = kv_mms(NCH - 1)
            L.wait("vector", "TP", kv_done[NCH - 1])
            L.wait("vector", "GP", kvstage_free_g)
            for bank in range(2):
                L.op("vector", "tensor_copy",
                     kvstage[:, bank * 512:(bank + 1) * 512], PB[6 + bank][:],
                     inc=("VP" if bank == 1 else None))
            v_kvst = L.val("VP")
            L.wait("gpsimd", "VP", v_kvst)
            L.op("gpsimd", "dma_start",
                 out=kvi[:, hbase * 128:hbase * 128 + 1024], in_=kvstage,
                 inc="GP", incn=16)
            return v_kvst, L.val("GP")

        v_kv1, g_kv1 = b2_half(0, RBh, lw_wv1, 0, 0)
        lw_wq = load_w(RBv, wqd, 0, 2048, "TP", L.val("TP"))
        v_kv2, g_kv2 = b2_half(8, RAh, lw_wv2, v_kv1, g_kv1)

        lw_wu4 = []
        wu_gate = L.val("TP")
        for qtr in range(4):
            dst = RAv[:, :, qtr * 512:(qtr + 1) * 512]
            lw_wu4.append(load_w(dst, wud, qtr * 512, 512,
                                 "TP" if qtr == 0 else None, wu_gate))
        L.wait("gpsimd", "GP", g_kv2)
        L.op("gpsimd", "collective_compute", "AllReduce", ALU.add,
             replica_groups=[list(range(NCORES))], ins=[kvi.opt()],
             outs=[kvo.opt()], inc="CC")
        cc_kv = L.val("CC")
        L.wait("gpsimd", "CC", cc_kv)
        L.op("gpsimd", "dma_start", out=kv_sb[:, :], in_=kvo, inc="LW", incn=16)
        lw_kv = L.val("LW")

        L.barrier()
        L.wait("tensor", "LW", lw_wq)
        prev = {"v_on": 0, "s_ss": 0, "s_u": 0, "v_r": 0, "r_evac": [0, 0],
                "t_rt": 0}
        pend_rt = []
        t_q_last = 0
        def flush_rt():
            lc0, p0, SPp, v_r0 = pend_rt.pop(0)
            L.wait("tensor", "VP", v_r0)
            for ic in range(16):
                L.op("tensor", "transpose",
                     PBb(SPp[2 + ic // 8])[:, (ic % 8) * 128:(ic % 8 + 1) * 128],
                     SCqfm[:, ic * 128:(ic + 1) * 128], idbf[:],
                     inc=("TP" if ic == 15 else None))
            t_rt0 = L.val("TP")
            L.wait("vector", "TP", t_rt0)
            for bank in range(2):
                dst = YFv[:, bank * 8:(bank + 1) * 8, lc0 * 128:(lc0 + 1) * 128]
                src2 = PBb(SPp[2 + bank])[:, 0:1024].rearrange(
                    "p (t c) -> p t c", c=128)
                L.op("vector", "tensor_copy", dst, src2,
                     inc=("VP" if bank == 1 else None))
            return L.val("VP")
        for lc in range(NCH):
            p = lc % 2
            SP_ = [0, 1, 2, 3] if p == 0 else [4, 5, 6, 7]
            SO_ = [4, 5, 6, 7] if p == 0 else [0, 1, 2, 3]
            L.wait("tensor", "VP", prev["v_on"])
            L.wait("tensor", "SP", prev["s_ss"])
            L.wait("tensor", "VP", prev["r_evac"][p])
            for b4 in range(4):
                for ic in range(16):
                    L.op("tensor", "matmul", PB[SP_[b4]][:],
                         YFv[:, ic, lc * 128:(lc + 1) * 128],
                         RBv[:, ic, b4 * 512:(b4 + 1) * 512],
                         start=(ic == 0), stop=(ic == 15),
                         inc=("TP" if (ic == 15 and b4 == 3) else None))
            t_q = L.val("TP")
            if lc == NCH - 1:
                t_q_last = t_q
            if pend_rt:
                r_evac_v = flush_rt()
                prev["r_evac"][1 - p] = r_evac_v
            L.wait("scalar", "TP", t_q)
            L.wait("scalar", "VP", prev["v_r"])
            for b4 in range(4):
                L.op("scalar", "activation", SCq[:, b4 * 512:(b4 + 1) * 512],
                     PB[SP_[b4]][:], AF.Relu, scale=INV_SQRT_DH,
                     inc=("SP" if b4 == 3 else None))
            s_q = L.val("SP")
            L.wait("tensor", "SP", s_q)
            for ic in range(16):
                L.op("tensor", "transpose",
                     PBb(SP_[ic // 8])[:, (ic % 8) * 128:(ic % 8 + 1) * 128],
                     SCq[:, ic * 128:(ic + 1) * 128], idbf[:],
                     inc=("TP" if ic == 15 else None))
            t_qt = L.val("TP")
            L.wait("vector", "TP", t_qt)
            for bank in range(2):
                L.op("vector", "tensor_copy",
                     SCqfm[:, bank * 1024:(bank + 1) * 1024],
                     PBb(SP_[bank])[:, 0:1024],
                     inc=("VP" if bank == 1 else None))
            v_qt = L.val("VP")
            def emit_o_block():
                L.wait("tensor", "VP", v_qt)
                L.wait("tensor", "SP", prev["s_u"])
                L.wait("tensor", "VP", prev["r_evac"][1 - p])
                if lc == 0:
                    L.wait("tensor", "LW", lw_kv)
                for h in range(H):
                    L.op("tensor", "matmul",
                         PB[SO_[h // 4]][:, (h % 4) * 128:(h % 4 + 1) * 128],
                         SCqfm[:, h * 128:(h + 1) * 128],
                         kv_sb[:, h * 128:(h + 1) * 128],
                         start=(h % 4 == 0), stop=(h % 4 == 3),
                         skip_group_check=True,
                         inc=("TP" if h == H - 1 else None))
                t_o = L.val("TP")
                L.wait("scalar", "TP", t_o)
                for h in range(H):
                    L.op("scalar", "activation", SCq[:, 0:128],
                         PB[SO_[h // 4]][:, (h % 4) * 128:(h % 4 + 1) * 128],
                         AF.Square, accum_out=SL(9)[:, h:h + 1],
                         inc=("SP" if h == H - 1 else None))
                s_ss = L.val("SP")
                L.wait("vector", "SP", s_ss)
                L.op("vector", "tensor_scalar", SL(11), SL(9), 1.0 / DH, 1e-30,
                     ALU.mult, ALU.max)
                dv()
                L.op("vector", "reciprocal", SL(10), SL(11), inc="VP")
                v_sd = L.val("VP")
                L.wait("scalar", "VP", v_sd)
                L.op("scalar", "activation", SL(10), SL(10), AF.Sqrt, inc="SP")
                s_sd = L.val("SP")
                L.wait("vector", "SP", s_sd)
                L.op("vector", "tensor_tensor", SL(12), SL(10), SL(10), ALU.mult)
                dv()
                L.op("vector", "tensor_tensor", SL(12), SL(12), SL(11), ALU.mult)
                dv()
                L.op("vector", "tensor_scalar", SL(12), SL(12), -0.5, 1.5,
                     ALU.mult, ALU.add)
                dv()
                L.op("vector", "tensor_tensor", SL(10), SL(10), SL(12), ALU.mult)
                dv()
                L.op("vector", "tensor_scalar", SL(10), SL(10), 1e12, None,
                     ALU.min, inc="VP")
                v_f2 = L.val("VP")
                dv()
                return t_o, s_ss, v_f2

            def emit_u_block():
                L.wait("tensor", "VP", v_qt)
                for b4 in range(4):
                    if lc == 0:
                        L.wait("tensor", "LW", lw_wu4[b4])
                    for ic in range(16):
                        L.op("tensor", "matmul", PB[SP_[b4]][:],
                             YFv[:, ic, lc * 128:(lc + 1) * 128],
                             RAv[:, ic, b4 * 512:(b4 + 1) * 512],
                             start=(ic == 0), stop=(ic == 15),
                             inc=("TP" if (ic == 15 and b4 == 3) else None))
                t_u2 = L.val("TP")
                L.wait("scalar", "TP", t_u2)
                ustrip2 = YFv[:, :, lc * 128:(lc + 1) * 128]
                for b in range(4):
                    L.op("scalar", "activation",
                         ustrip2[:, b * 4:(b + 1) * 4, :],
                         PB[SP_[b]][:].rearrange("p (a c) -> p a c", c=128),
                         AF.Silu, inc=("SP" if b == 3 else None))
                return t_u2, L.val("SP"), ustrip2

            if lc == 0:
                t_u, s_u, ustrip = emit_u_block()
                t_o, s_ss, v_f = emit_o_block()
            else:
                t_o, s_ss, v_f = emit_o_block()
                t_u, s_u, ustrip = emit_u_block()
            L.wait("vector", "VP", v_f)
            for b in range(4):
                fb = SL(10)[:, b * 4:(b + 1) * 4].unsqueeze(2).broadcast_to(
                    [128, 4, 128])
                L.op("vector", "tensor_tensor",
                     SCq[:, b * 512:(b + 1) * 512].rearrange(
                         "p (a c) -> p a c", c=128),
                     PB[SO_[b]][:].rearrange("p (a c) -> p a c", c=128),
                     fb, ALU.mult, inc=("VP" if b == 3 else None))
            v_on = L.val("VP")
            L.wait("vector", "SP", s_u)
            L.wait("vector", "SP", s_u)
            L.op("vector", "tensor_tensor",
                 SCqfm[:, :].rearrange("p (a c) -> p a c", c=128),
                 SCq[:, :].rearrange("p (a c) -> p a c", c=128),
                 ustrip, ALU.mult, inc="VP")
            v_r = L.val("VP")
            pend_rt.append((lc, p, SP_, v_r))
            prev = {"v_on": v_on, "s_ss": s_ss, "s_u": s_u, "v_r": v_r,
                    "r_evac": prev["r_evac"], "t_rt": prev["t_rt"]}
        while pend_rt:
            flush_rt()
        v_b34 = L.val("VP")

        lw_wo4 = []
        for qtr in range(4):
            dst = RBv[:, :, qtr * 512:(qtr + 1) * 512]
            lw_wo4.append(load_w(dst, wod, qtr * 512, 512,
                                 "TP" if qtr == 0 else None, t_q_last))

        L.barrier()
        b5_xin_rel = [0, 0]
        b5_out_rel = [0, 0]
        b5_tb = {b: 0 for b in range(8)}
        b5_marks = []
        b34_last_tp = L.val("TP")
        for lc in range(NCH):
            sl = lc % 2
            if lc < 2:
                L.wait("sync", "TP", b34_last_tp)
            L.wait("sync", "VP", b5_xin_rel[sl])
            L.op("sync", "dma_start", out=B5_xin(sl),
                 in_=x1b[lc * 128:(lc + 1) * 128, :], inc="LX", incn=16)
            ld = L.val("LX")
            banks = [(lc * 4 + ot) % 8 for ot in range(4)]
            for b in banks:
                L.wait("tensor", "VP", b5_tb[b])
            for ot in range(4):
                if lc == 0:
                    L.wait("tensor", "LW", lw_wo4[ot])
                for ic in range(16):
                    L.op("tensor", "matmul", PB[banks[ot]][:],
                         YFv[:, ic, lc * 128:(lc + 1) * 128],
                         RBv[:, ic, ot * 512:(ot + 1) * 512],
                         start=(ic == 0), stop=(ic == 15),
                         inc=("TP" if (ic == 15 and ot == 3) else None))
            t_mm = L.val("TP")
            L.wait("vector", "TP", t_mm)
            L.wait("vector", "LX", ld)
            L.wait("vector", "SO", b5_out_rel[sl])
            for ot in range(4):
                L.op("vector", "tensor_tensor",
                     B5_out(sl)[:, ot * 512:(ot + 1) * 512], PB[banks[ot]][:],
                     B5_xin(sl)[:, ot * 512:(ot + 1) * 512], ALU.add, inc="VP")
                b5_tb[banks[ot]] = L.val("VP")
            v_x2 = L.val("VP")
            b5_xin_rel[sl] = v_x2
            L.op("vector", "tensor_reduce", SL(0)[:, lc:lc + 1], B5_out(sl),
                 AX.X, ALU.add)
            L.op("vector", "tensor_reduce", SL(2)[:, lc:lc + 1], B5_out(sl),
                 AX.X, ALU.max)
            L.op("vector", "tensor_reduce", SL(3)[:, lc:lc + 1], B5_out(sl),
                 AX.X, ALU.min, inc="VP")
            v_st = L.val("VP")
            L.wait("scalar", "VP", v_st)
            L.op("scalar", "activation", junk_b5[:, :], B5_out(sl), AF.Square,
                 accum_out=SL(1)[:, lc:lc + 1], inc="SP")
            s_st = L.val("SP")
            L.wait("scalar", "SP", s_st)
            L.op("scalar", "dma_start", out=x2b[lc * 128:(lc + 1) * 128, :],
                 in_=B5_out(sl), inc="SO", incn=16)
            b5_out_rel[sl] = L.val("SO")
            b5_marks.append(b5_out_rel[sl])
            L.wait("scalar", "TP", t_mm)
            src = wtsp.rearrange("p (a b) -> p a b", b=D)[:, :, lc * 128:(lc + 1) * 128]
            L.op("scalar", "dma_start", out=WT8_C[:, :, lc * 128:(lc + 1) * 128],
                 in_=src, inc="LR", incn=16)
        lw_wt_reload = L.val("LR")
        b5_last_tp = L.val("TP")

        if STAGE == "B":
            L.wait("gpsimd", "SO", L.val("SO"))
            L.op("gpsimd", "dma_start", out=outd, in_=x2b, inc="SO", incn=16)
            final_so = L.val("SO")
        else:
            L.barrier()
            marks_c = batch_ln_finalize(True, mx3i, mx3o)
            L.barrier()
            L.wait("tensor", "LR", lw_wt_reload)
            L.wait("sync", "TP", b5_last_tp)
            L.op("sync", "dma_start", out=C_out(0)[0:1, 0:2048], in_=bd,
                 inc="LX", incn=16)
            ld_bias_c = L.val("LX")
            L.wait("vector", "LX", ld_bias_c)
            L.op("vector", "tensor_scalar", biasrow, C_out(0)[0:1, 0:2048], 0.5,
                 None, ALU.mult, inc="VP")
            v_biasrow_c = L.val("VP")
            pipe["tr"] = L.val("TP")
            pipe["trevac"] = ("VP", marks_c["psum0_free"])
            pipe["y2t"] = 0
            pipe["y2evac"] = ("VP", 0)
            c_guards = [[("TP", b5_last_tp)], [("TP", b5_last_tp)]]
            ffn_phase(x2b, outd, WT8_C, C_xin, C_out, False, marks_c["rstd_s"],
                      b5_marks, v_biasrow_c, c_guards)
            final_so = L.val("SO")

    L.wait("gpsimd", "SO", final_so)
    if DEBUG:
        dstt = nc.dram_tensor("dstt", [128, 13 * 16], F32, kind="ExternalOutput").ap()
        dgmax = nc.dram_tensor("dgmax", [128, 1], F32, kind="ExternalOutput").ap()
        dws = nc.dram_tensor("dws", [128, 1], F32, kind="ExternalOutput").ap()
        dscr = nc.dram_tensor("dscr", [128, 7168], F32, kind="ExternalOutput").ap()
        dra = nc.dram_tensor("dra", [128, 16 * D], F32, kind="ExternalOutput").ap()
        L.op("gpsimd", "dma_start", out=dstt, in_=stt[:], inc="SO", incn=16)
        L.op("gpsimd", "dma_start", out=dgmax, in_=gmax_bc[:], inc="SO", incn=16)
        L.op("gpsimd", "dma_start", out=dws, in_=ws_bc[:], inc="SO", incn=16)
        L.op("gpsimd", "dma_start", out=dscr, in_=SCR[:], inc="SO", incn=16)
        L.op("gpsimd", "dma_start", out=dra, in_=RA[:], inc="SO", incn=16)
        L.wait("gpsimd", "SO", L.val("SO"))
    L.op("gpsimd", "nop", inc="GP")

    sems = {}
    for name in L.sem_names:
        sems[name] = ctx.enter_context(nc.semaphore(name))
    block = ctx.enter_context(nc.Block())

    def runner(eng_name):
        def run(e):
            for fn in L.ops[eng_name]:
                fn(e, sems)
        return run

    for eng_name in ENGINES:
        getattr(block, eng_name)(runner(eng_name))

    ctx.close()
    n_instr = sum(len(v) for v in L.ops.values())
    return nc, n_instr


def kernel(L0_attn_x, L0_attn_Wq, L0_attn_Wk, L0_attn_Wv, L0_attn_Wu,
           L0_attn_Wo, L0_ffn_weight, L0_ffn_weight_scale, L0_ffn_bias):
    x = np.ascontiguousarray(
        np.asarray(L0_attn_x, dtype=np.float32).reshape(16384, 2048))
    shards = [np.ascontiguousarray(x[c * T:(c + 1) * T]) for c in range(NCORES)]
    wq = np.ascontiguousarray(np.asarray(L0_attn_Wq, dtype=np.float32))
    wk = np.ascontiguousarray(np.asarray(L0_attn_Wk, dtype=np.float32))
    wv = np.ascontiguousarray(np.asarray(L0_attn_Wv, dtype=np.float32))
    wu = np.ascontiguousarray(np.asarray(L0_attn_Wu, dtype=np.float32))
    wo = np.ascontiguousarray(np.asarray(L0_attn_Wo, dtype=np.float32))
    wf = np.ascontiguousarray(np.asarray(L0_ffn_weight, dtype=np.float32))
    ws = np.ascontiguousarray(np.asarray(L0_ffn_weight_scale, dtype=np.float32).reshape(1, 1))
    bias = np.ascontiguousarray(np.asarray(L0_ffn_bias, dtype=np.float32).reshape(1, 2048))
    idbf = np.eye(128, dtype=ml_dtypes.bfloat16)
    idf8 = np.eye(128, dtype=ml_dtypes.float8_e4m3)
    onesb = np.ones((1, 128), dtype=ml_dtypes.bfloat16)

    nc, _ = build_program()
    in_maps = [{
        "x0": shards[c], "wq": wq, "wk": wk, "wv": wv, "wu": wu, "wo": wo,
        "wf": wf, "ws": ws, "bias": bias, "idbf": idbf, "idf8": idf8,
        "onesb": onesb,
    } for c in range(NCORES)]
    res = run_bass_kernel_spmd(nc, in_maps, core_ids=list(range(NCORES)))
    out = np.concatenate([res.results[c]["out"] for c in range(NCORES)], axis=0)
    return out.reshape(1, 16384, 2048)
